# version 09
# baseline (speedup 1.0000x reference)
import contextlib
import numpy as np
import ml_dtypes
import concourse.bass as bass
import concourse.mybir as mybir
from concourse.bass_utils import run_bass_kernel_spmd

F32 = mybir.dt.float32
BF16 = mybir.dt.bfloat16
AF = mybir.ActivationFunctionType
ALU = mybir.AluOpType
AX = mybir.AxisListType.X

NCORE = 8
T = 1024
D = 2048
NK = 16
NJ = 64
NH = 16
NKV = 4
EPS = 1e-5
MASK_NEG = 1.0e4
ISQ = 0.08838834764831845


class Prog:
    def __init__(self, nc, stack, blk):
        self.nc = nc
        self.blk = blk
        names = ["WD0", "WD1", "WD2", "WD3", "WC", "XD0", "XD1", "OD0",
                 "OD1", "OH0", "OH1", "XDI", "ODM", "PE", "AC", "VE", "GD",
                 "GP"]
        self.sem = {n: stack.enter_context(nc.semaphore(n)) for n in names}
        self.cnt = {n: 0 for n in names}

    def inc(self, name, inst, k=1):
        inst.then_inc(self.sem[name], k)
        self.cnt[name] += k
        return self.cnt[name]

    def wait(self, eng, name, val=None):
        v = self.cnt[name] if val is None else val
        if v > 0:
            eng.wait_ge(self.sem[name], v)

    def seq_dma(self, name, eng, dst, src):
        self.wait(eng, name)
        return self.inc(name, eng.dma_start(dst, src), 16)


class DStream:

    def __init__(self, p, s0, s1):
        self.p = p
        self.names = (s0, s1)
        self.n = 0

    def dma(self, eng, dst, src):
        i = self.n
        self.p.seq_dma(self.names[i % 2], eng, dst, src)
        self.n += 1
        return i

    def wait_chunk(self, eng, i):
        self.p.wait(eng, self.names[i % 2], 16 * (i // 2 + 1))

    def wait_all(self, eng):
        self.p.wait(eng, self.names[0])
        self.p.wait(eng, self.names[1])


class WStream:

    NBUF = 2

    def __init__(self, p, wbuf):
        self.p = p
        self.wbuf = wbuf
        self.n_dma = 0
        self.n_used = 0

    def load_run(self, parent, count):
        assert count % 2 == 0
        p = self.p

        @p.blk.gpsimd
        def _(g):
            for c2 in range(count // 2):
                i = self.n_dma
                P = i // 2
                if P >= self.NBUF:
                    p.wait(g, "WC", 2 * (P - self.NBUF) + 2)
                dst = self.wbuf[P % self.NBUF][:].rearrange(
                    "p (c t) -> p c t", c=2)
                srcp = parent[2 * c2:2 * c2 + 2].rearrange("c p t -> p c t")
                p.seq_dma("WD%d" % (P % self.NBUF), g, dst, srcp)
                self.n_dma += 2

    def acquire(self, eng):
        i = self.n_used
        P = i // 2
        self.p.wait(eng, "WD%d" % (P % self.NBUF), 16 * (P // self.NBUF + 1))
        return self.wbuf[P % self.NBUF][:, 2048 * (i % 2):2048 * (i % 2 + 1)]

    def release(self, inst):
        self.p.inc("WC", inst, 1)
        self.n_used += 1


def emit_stats_mms(p, t, PS, onesb, xb, xq, first, last):
    for x in range(2):
        sl = slice(512 * x, 512 * (x + 1))
        t.matmul(PS[4 + x][0:1, :], onesb[:], xb[:, sl], start=first, stop=last)
        mm = t.matmul(PS[6 + x][0:1, :], onesb[:], xq[:, sl],
                      start=first, stop=last)
    return mm


def emit_ln_rows(p, sb, pe_stats_done):
    blk, PS = p.blk, sb["PS"]
    rmean, rvar, rR = sb["rows"]

    @blk.vector
    def _(v):
        p.wait(v, "PE", pe_stats_done)
        for x in range(2):
            sl = slice(512 * x, 512 * (x + 1))
            v.tensor_scalar(rmean[:, sl], PS[4 + x][0:1, :], 1.0 / D, None,
                            ALU.mult)
            v.tensor_scalar(rvar[:, sl], PS[6 + x][0:1, :], 1.0 / D, None,
                            ALU.mult)
        v.drain()
        v.tensor_tensor(rR[:, :], rmean[:, :], rmean[:, :], ALU.mult)
        v.drain()
        inst = v.tensor_tensor(rvar[:, :], rvar[:, :], rR[:, :], ALU.subtract)
        p.inc("VE", inst)
    ve_var = p.cnt["VE"]

    @blk.scalar
    def _(s):
        p.wait(s, "VE", ve_var)
        s.activation(rR[:, :], rvar[:, :], AF.Ln, bias=sb["epst"][:], scale=1.0)
        s.drain()
        inst = s.activation(rR[:, :], rR[:, :], AF.Exp, bias=0.0, scale=-0.5)
        p.inc("AC", inst)
    ac_rstd = p.cnt["AC"]

    @blk.vector
    def _(v):
        p.wait(v, "AC", ac_rstd)
        inst = v.scalar_tensor_tensor(rmean[:, :], rmean[:, :], -1.0, rR[:, :],
                                      ALU.mult, ALU.mult)
        p.inc("VE", inst)
    ve_b = p.cnt["VE"]

    @blk.tensor
    def _(t):
        p.wait(t, "VE", ve_b)
        for x in range(2):
            sl = slice(512 * x, 512 * (x + 1))
            t.matmul(PS[4 + x][:, :], sb["onesPf"][:], rR[:, sl],
                     start=True, stop=True)
            mm = t.matmul(PS[6 + x][:, :], sb["onesPf"][:], rmean[:, sl],
                          start=True, stop=True)
        p.inc("PE", mm)
    pe_bc = p.cnt["PE"]

    @blk.scalar
    def _(s):
        p.wait(s, "PE", pe_bc)
        p.wait(s, "OD0")
        p.wait(s, "OD1")
        for x in range(2):
            sl = slice(512 * x, 512 * (x + 1))
            s.copy(sb["Rt"][:, sl], PS[4 + x][:, :])
            inst = s.copy(sb["Bt"][:, sl], PS[6 + x][:, :])
        p.inc("AC", inst)
    return p.cnt["AC"]


def emit_ln_simple(p, sb, xs, x_hbm):
    blk, PS, XH = p.blk, sb["PS"], sb["XH"]
    xsq, fscr = sb["xsq"], sb["fscr"]
    stg = sb["vf"][:, 0:NK * T]
    ve0, pe0 = p.cnt["VE"], p.cnt["PE"]
    gp0 = p.cnt["GP"]
    xb0 = xs.n

    @blk.sync
    def _(sync):
        for i in range(4):
            xs.dma(sync,
                   stg[:, 4096 * i:4096 * (i + 1)].rearrange(
                       "p (c t) -> p c t", c=4),
                   x_hbm[4 * i:4 * (i + 1)].rearrange("c p t -> p c t"))

    def stgc(k):
        return stg[:, T * k:T * (k + 1)]

    @blk.vector
    def _(v):
        for k in range(NK):
            xs.wait_chunk(v, xb0 + k // 4)
            inst = v.tensor_copy(XH[:, T * k:T * (k + 1)], stgc(k))
            p.inc("VE", inst)

    @blk.gpsimd
    def _(g):
        for k in range(NK):
            xs.wait_chunk(g, xb0 + k // 4)
            if k >= 2:
                p.wait(g, "PE", pe0 + (k - 1))
            inst = g.tensor_tensor(xsq[k % 2][:], stgc(k), stgc(k), ALU.mult)
            g.drain()
            inst = g.engine_nop()
            p.inc("GP", inst)

    @blk.tensor
    def _(t):
        for k in range(NK):
            p.wait(t, "VE", ve0 + (k + 1))
            p.wait(t, "GP", gp0 + (k + 1))
            mm = emit_stats_mms(p, t, PS, sb["onesb"],
                                XH[:, T * k:T * (k + 1)], xsq[k % 2],
                                k == 0, k == NK - 1)
            p.inc("PE", mm)
    pe_stats = p.cnt["PE"]

    ac_rows = emit_ln_rows(p, sb, pe_stats)

    ve1 = p.cnt["VE"]
    gp1 = p.cnt["GP"]
    SPL = 384

    @blk.vector
    def _(v):
        p.wait(v, "AC", ac_rows)
        for k in range(NK):
            if k >= 2:
                p.wait(v, "GP", gp1 + (k - 1))
            inst = v.tensor_tensor(fscr[k % 2][:], stgc(k),
                                   sb["Rt"][:, :], ALU.mult)
            p.inc("VE", inst)
            v.drain()
            inst = v.tensor_tensor(XH[:, T * k:T * k + SPL],
                                   fscr[k % 2][:, 0:SPL],
                                   sb["Bt"][:, 0:SPL], ALU.add)
            p.inc("VE", inst)

    @blk.gpsimd
    def _(g):
        for k in range(NK):
            p.wait(g, "VE", ve1 + 2 * k + 1)
            inst = g.tensor_tensor(XH[:, T * k + SPL:T * (k + 1)],
                                   fscr[k % 2][:, SPL:1024],
                                   sb["Bt"][:, SPL:1024], ALU.add)
            g.drain()
            inst = g.engine_nop()
            p.inc("GP", inst)

    return lambda k: [("VE", ve1 + 2 * (k + 1)), ("GP", gp1 + (k + 1))]


def emit_ffn(p, sb, ws, xs, od, w0_chunks, w1_chunks, x_res_hbm, out_hbm,
             want_stats, tag="ffn", xh_ready=None):
    blk, PS, XS = p.blk, sb["PS"], sb["XS"]
    Dreg, XH = sb["Dreg"], sb["XH"]
    etmp, x1t = sb["etmp"], sb["x1t"]

    ws.load_run(w0_chunks, NJ)
    ve_xhat = p.cnt["VE"]
    ac0 = p.cnt["AC"]
    wcA0 = ws.n_used
    scope_a = p.nc.named_scope(tag + "_A")
    scope_a.__enter__()

    gp_xhat = p.cnt["GP"]

    @blk.tensor
    def _(t):
        if xh_ready is None:
            p.wait(t, "VE", ve_xhat)
            p.wait(t, "GP", gp_xhat)
        p.wait(t, "AC", ac0)
        for j in range(NJ):
            w = ws.acquire(t)
            if j >= 2:
                p.wait(t, "AC", ac0 + (j - 1))
            a, b = PS[2 * (j % 2)], PS[2 * (j % 2) + 1]
            for k in range(NK):
                if j == 0 and xh_ready is not None:
                    for semn, val in xh_ready(k):
                        p.wait(t, semn, val)
                t.matmul(a[:], w[:, 128 * k:128 * (k + 1)],
                         XH[:, T * k:T * k + 512],
                         start=(k == 0), stop=(k == NK - 1))
                mm = t.matmul(b[:], w[:, 128 * k:128 * (k + 1)],
                              XH[:, T * k + 512:T * (k + 1)],
                              start=(k == 0), stop=(k == NK - 1))
            ws.release(mm)

    @blk.scalar
    def _(s):
        for j in range(NJ):
            p.wait(s, "WC", wcA0 + j + 1)
            a, b = PS[2 * (j % 2)], PS[2 * (j % 2) + 1]
            s.activation(Dreg[:, T * j:T * j + 512], a[:], AF.Gelu,
                         bias=sb["b0t"][:, j:j + 1], scale=1.0)
            inst = s.activation(Dreg[:, T * j + 512:T * (j + 1)], b[:], AF.Gelu,
                                bias=sb["b0t"][:, j:j + 1], scale=1.0)
            p.inc("AC", inst)
    ac_phA = p.cnt["AC"]
    scope_a.__exit__(None, None, None)
    scope_b = p.nc.named_scope(tag + "_B")
    scope_b.__enter__()

    for w1h in w1_chunks:
        ws.load_run(w1h, 4)
    veB = p.cnt["VE"]
    wcB0 = ws.n_used
    vper = 3 if want_stats else 1
    pe_S = [None] * NH
    xbB = xs.n
    obB = od.n

    @blk.sync
    def _(sync):
        for h in range(NH):
            if h >= 2:
                p.wait(sync, "VE", veB + vper * (h - 1))
            else:
                p.wait(sync, "VE", veB)
            xs.dma(sync, XS[h % 2][:], x_res_hbm[h])

    @blk.tensor
    def _(t):
        p.wait(t, "AC", ac_phA)
        for h in range(NH):
            a, b = PS[2 * (h % 2)], PS[2 * (h % 2) + 1]
            if h >= 2:
                p.wait(t, "AC", ac_phA + (h - 1))
            for jg in range(4):
                w = ws.acquire(t)
                for k in range(NK):
                    j = 16 * jg + k
                    st, sp = (j == 0), (j == NJ - 1)
                    t.matmul(a[:], w[:, 128 * k:128 * (k + 1)],
                             Dreg[:, T * j:T * j + 512], start=st, stop=sp)
                    mm = t.matmul(b[:], w[:, 128 * k:128 * (k + 1)],
                                  Dreg[:, T * j + 512:T * (j + 1)],
                                  start=st, stop=sp)
                ws.release(mm)
            if want_stats and h >= 1:
                p.wait(t, "VE", veB + vper * h)
                mm = emit_stats_mms(p, t, PS, sb["onesb"],
                                    XH[:, T * (h - 1):T * h],
                                    sb["xsq"][(h - 1) % 2],
                                    h - 1 == 0, False)
                pe_S[h - 1] = p.inc("PE", mm)
        if want_stats:
            p.wait(t, "VE", veB + vper * NH)
            mm = emit_stats_mms(p, t, PS, sb["onesb"],
                                XH[:, T * (NH - 1):T * NH],
                                sb["xsq"][(NH - 1) % 2], False, True)
            pe_S[NH - 1] = p.inc("PE", mm)

    @blk.scalar
    def _(s):
        for h in range(NH):
            if h >= 2:
                p.wait(s, "VE", veB + vper * (h - 2) + 1)
                od.dma(s, out_hbm[h - 2], x1t[(h - 2) % 2][:])
            p.wait(s, "WC", wcB0 + 4 * (h + 1))
            if h >= 2:
                p.wait(s, "VE", veB + vper * (h - 1))
            a, b = PS[2 * (h % 2)], PS[2 * (h % 2) + 1]
            et = etmp[h % 2]
            s.activation(et[:, 0:512], a[:], AF.Copy, bias=0.0, scale=0.5)
            inst = s.activation(et[:, 512:1024], b[:], AF.Copy, bias=0.0,
                                scale=0.5)
            p.inc("AC", inst)
        for h in (NH - 2, NH - 1):
            p.wait(s, "VE", veB + vper * h + 1)
            od.dma(s, out_hbm[h], x1t[h % 2][:])

    @blk.vector
    def _(v):
        for h in range(NH):
            p.wait(v, "AC", ac_phA + (h + 1))
            xs.wait_chunk(v, xbB + h)
            p.wait(v, "WC", wcB0 + 4 * (h + 1))
            if h >= 2:
                od.wait_chunk(v, obB + h - 2)
                if want_stats:
                    p.wait(v, "PE", pe_S[h - 2])
            inst = v.scalar_tensor_tensor(x1t[h % 2][:], etmp[h % 2][:],
                                          sb["hb1t"][:, h:h + 1],
                                          XS[h % 2][:], ALU.add, ALU.add)
            p.inc("VE", inst)
            if want_stats:
                v.drain()
                v.tensor_copy(XH[:, T * h:T * (h + 1)], x1t[h % 2][:])
                v.drain()
                inst = v.tensor_tensor(sb["xsq"][h % 2][:], x1t[h % 2][:],
                                       x1t[h % 2][:], ALU.mult)
                p.inc("VE", inst, 2)

    scope_b.__exit__(None, None, None)
    return pe_S[NH - 1] if want_stats else None


def emit_proj_rope(p, sb, ws, w_chunks, nout, dst, do_rope):
    blk, PS = p.blk, sb["PS"]
    XH, fscr = sb["XH"], sb["fscr"]
    ws.load_run(w_chunks, nout)
    ac0 = p.cnt["AC"]
    wcP0 = ws.n_used
    gpP0 = p.cnt["GP"]

    @blk.tensor
    def _(t):
        p.wait(t, "AC", ac0)
        p.wait(t, "GP", gpP0)
        for i in range(nout):
            w = ws.acquire(t)
            if i >= 2:
                p.wait(t, "AC", ac0 + 2 * (i - 1))
            a, b = PS[2 * (i % 2)], PS[2 * (i % 2) + 1]
            for k in range(NK):
                t.matmul(a[:], w[:, 128 * k:128 * (k + 1)],
                         XH[:, T * k:T * k + 512],
                         start=(k == 0), stop=(k == NK - 1))
                mm = t.matmul(b[:], w[:, 128 * k:128 * (k + 1)],
                              XH[:, T * k + 512:T * (k + 1)],
                              start=(k == 0), stop=(k == NK - 1))
            ws.release(mm)

    @blk.scalar
    def _(s):
        for i in range(nout):
            p.wait(s, "WC", wcP0 + i + 1)
            a, b = PS[2 * (i % 2)], PS[2 * (i % 2) + 1]
            s.copy(dst[:, T * i:T * i + 512], a[:])
            inst = s.copy(dst[:, T * i + 512:T * (i + 1)], b[:])
            p.inc("AC", inst, 2)
    ac_ev = p.cnt["AC"]

    if not do_rope:
        return

    peR0 = p.cnt["PE"]
    veR0 = p.cnt["VE"]

    @blk.tensor
    def _(t):
        p.wait(t, "GD")
        for idx in range(2 * nout):
            i, x = divmod(idx, 2)
            p.wait(t, "AC", ac0 + 2 * (i + 1))
            if idx >= 2:
                p.wait(t, "VE", veR0 + 3 * (idx - 1))
            mm = t.matmul(PS[4 + (idx % 2)][:, :], sb["pmb"][:],
                          dst[:, T * i + 512 * x:T * i + 512 * (x + 1)],
                          start=True, stop=True)
            p.inc("PE", mm)

    @blk.vector
    def _(v):
        for idx in range(2 * nout):
            i, x = divmod(idx, 2)
            p.wait(v, "PE", peR0 + idx + 1)
            sl = slice(T * i + 512 * x, T * i + 512 * (x + 1))
            csl = slice(512 * x, 512 * (x + 1))
            v.drain()
            v.tensor_tensor(fscr[0][:, 0:512], dst[:, sl],
                            sb["cosT"][:, csl], ALU.mult)
            v.tensor_tensor(fscr[1][:, 0:512], PS[4 + (idx % 2)][:, :],
                            sb["sinT"][:, csl], ALU.mult)
            v.drain()
            inst = v.tensor_tensor(dst[:, sl], fscr[0][:, 0:512],
                                   fscr[1][:, 0:512], ALU.add)
            p.inc("VE", inst, 3)


def emit_kproj_chunked(p, sb, ws, wkvm, ktmp, xh_ready):
    blk, PS = p.blk, sb["PS"]
    XH = sb["XH"]
    ws.load_run(wkvm, 4)
    ac0 = p.cnt["AC"]
    wcK0 = ws.n_used

    @blk.tensor
    def _(t):
        p.wait(t, "AC", ac0)
        for g in range(4):
            w = ws.acquire(t)
            for j in range(4):
                kc = 4 * g + j
                for semn, val in xh_ready(kc):
                    p.wait(t, semn, val)
                for o in range(4):
                    wt = w[:, 512 * j + 128 * o:512 * j + 128 * (o + 1)]
                    t.matmul(PS[2 * o][:, :], wt, XH[:, T * kc:T * kc + 512],
                             start=(kc == 0), stop=(kc == NK - 1))
                    mm = t.matmul(PS[2 * o + 1][:, :], wt,
                                  XH[:, T * kc + 512:T * (kc + 1)],
                                  start=(kc == 0), stop=(kc == NK - 1))
            p.inc("WC", mm, 1)
            ws.n_used += 1

    @blk.scalar
    def _(s):
        p.wait(s, "WC", wcK0 + 4)
        for idx in range(8):
            o, x = divmod(idx, 2)
            inst = s.copy(ktmp[idx], PS[2 * o + x][:, :])
        p.inc("AC", inst)


def emit_krope(p, sb, ktmp, dst):
    blk, PS, fscr = p.blk, sb["PS"], sb["fscr"]
    ac_ev = p.cnt["AC"]
    peR0 = p.cnt["PE"]
    veR0 = p.cnt["VE"]
    xdiR = p.cnt["XDI"]

    @blk.tensor
    def _(t):
        p.wait(t, "GD")
        p.wait(t, "AC", ac_ev)
        for idx in range(8):
            if idx >= 2:
                p.wait(t, "VE", veR0 + 3 * (idx - 1))
            mm = t.matmul(PS[4 + (idx % 2)][:, :], sb["pmb"][:],
                          ktmp[idx], start=True, stop=True)
            p.inc("PE", mm)

    @blk.vector
    def _(v):
        p.wait(v, "XDI", xdiR)
        for idx in range(8):
            i, x = divmod(idx, 2)
            p.wait(v, "PE", peR0 + idx + 1)
            sl = slice(T * i + 512 * x, T * i + 512 * (x + 1))
            csl = slice(512 * x, 512 * (x + 1))
            v.drain()
            v.tensor_tensor(fscr[0][:, 0:512], ktmp[idx],
                            sb["cosT"][:, csl], ALU.mult)
            v.tensor_tensor(fscr[1][:, 0:512], PS[4 + (idx % 2)][:, :],
                            sb["sinT"][:, csl], ALU.mult)
            v.drain()
            inst = v.tensor_tensor(dst[:, sl], fscr[0][:, 0:512],
                                   fscr[1][:, 0:512], ALU.add)
            p.inc("VE", inst, 3)


def make_sbufs(nc):
    A = lambda n, s, dt: nc.alloc_sbuf_tensor(n, s, dt).ap()
    sb = {}
    sb["Dreg"] = A("Dreg", [128, NJ * T], BF16)
    sb["XH"] = A("XH", [128, NK * T], BF16)
    sb["XS"] = [A(f"XS{i}", [128, T], F32) for i in range(2)]
    sb["wbuf"] = [A(f"wbuf{i}", [128, 4096], BF16) for i in range(2)]
    sb["xsq"] = [A(f"xsq{i}", [128, T], BF16) for i in range(2)]
    sb["fscr"] = [A(f"fscr{i}", [128, T], F32) for i in range(2)]
    sb["etmp"] = sb["fscr"]
    sb["x1t"] = [A(f"x1t{i}", [128, T], F32) for i in range(2)]
    sb["Rt"] = sb["x1t"][0]
    sb["Bt"] = sb["x1t"][1]
    sb["onesb"] = A("onesb", [128, 1], BF16)
    sb["onesPf"] = A("onesPf", [1, 128], F32)
    sb["b0t"] = A("b0t", [128, NJ], F32)
    sb["b1t"] = A("b1t", [128, NH], F32)
    sb["hb1t"] = A("hb1t", [128, NH], F32)
    sb["pmb"] = A("pmb", [128, 128], BF16)
    sb["epst"] = A("epst", [1, 1], F32)
    sb["PS"] = [nc.alloc_psum_tensor(f"P{i}", [128, 512], F32).ap()
                for i in range(8)]
    vf = sb["Dreg"].bitcast(F32)
    sb["vf"] = vf
    sb["rows"] = [vf[0:1, 25088 + 1024 * i:25088 + 1024 * (i + 1)]
                  for i in range(3)]
    sb["cosT"] = vf[:, 28160:29184]
    sb["sinT"] = vf[:, 29184:30208]
    return sb


def emit_init(p, sb, loads_f32, loads_cast):
    blk = p.blk

    @blk.sync
    def _(sync):
        for dst, src in loads_f32:
            p.seq_dma("XDI", sync, dst[:], src)
    init_xd = p.cnt["XDI"]

    @blk.gpsimd
    def _(g):
        for dst, src in loads_cast:
            p.seq_dma("GD", g, dst[:], src)

    @blk.vector
    def _(v):
        v.memset(sb["onesb"][:], 1.0)
        v.memset(sb["onesPf"][:], 1.0)
        v.memset(sb["epst"][:], EPS)
        p.wait(v, "XDI", init_xd)
        inst = v.tensor_scalar(sb["hb1t"][:], sb["b1t"][:], 0.5, None, ALU.mult)
        p.inc("VE", inst)


def build_launch1():
    nc = bass.Bass("TRN2", target_bir_lowering=False, debug=False)
    f = lambda n, s, dt=F32: nc.dram_tensor(n, s, dt, kind="ExternalInput").ap()
    o = lambda n, s, dt=F32: nc.dram_tensor(n, s, dt, kind="ExternalOutput").ap()

    x0 = f("x0", [NK, 128, T])
    w0m = f("w0m", [NJ, 128, 2048])
    w1m = f("w1m", [NH, 4, 128, 2048])
    b0c = f("b0c", [128, NJ])
    b1c = f("b1c", [128, NH])
    wkvm = f("wkvm", [4, 128, 2048])
    wvm = f("wvm", [NKV, 128, 2048])
    wiqm = f("wiqm", [128, 2048])
    wikm = f("wikm", [128, 2048])
    cwab = f("cwab", [128, 2])
    cost = f("cost", [128, T])
    sint = f("sint", [128, T])
    pmat = f("pmat", [128, 128])

    x1_out = o("x1_out", [NH, 128, T])
    xh1_out = o("xh1_out", [NK, 128, T], BF16)
    ck_out = o("ck_out", [128, NKV * 64], BF16)
    cv_out = o("cv_out", [128, NKV * 64], BF16)
    ki_out = o("ki_out", [128, 1])
    qi_out = o("qi_out", [128, 512])

    sb = make_sbufs(nc)
    A = lambda n, s, dt: nc.alloc_sbuf_tensor(n, s, dt).ap()
    cwb = A("cwb", [128, 2], BF16)
    xsum = A("xsum", [128, NK], F32)
    kiT = A("kiT", [128, 1], F32)
    vf = sb["vf"]
    Dg = sb["Dreg"]
    xhf = vf[:, 0:NK * 512]
    krot = Dg[:, 0:NKV * T]
    vreg = Dg[:, NKV * T:2 * NKV * T]
    wiqt = vf[:, 8192:10240]
    wikt = vf[:, 10240:12288]
    eaT = vf[0:1, 12288:16384]
    ebT = vf[0:1, 16384:20480]
    wrow = vf[0:1, 20480:24576]
    cks = vf[:, 24576:24832]
    cvs = vf[:, 24832:25088]
    qiT = vf[:, 30208:30720]
    ztr = [vf[0:1, 25088:29184]]
    za = vf[0:1, 30720:30976]
    zb = vf[0:1, 30976:31232]
    zsc = vf[0:1, 31232:31488]
    wt1 = vf[0:1, 31488:31744]
    wt2 = vf[0:1, 31744:32000]
    ckb = Dg[:, 64000:64256]
    cvb = Dg[:, 64256:64512]

    with contextlib.ExitStack() as stack:
        blk = stack.enter_context(nc.Block())
        p = Prog(nc, stack, blk)
        ws = WStream(p, sb["wbuf"])
        PS = sb["PS"]

        emit_init(p, sb,
                  [(sb["b0t"], b0c), (sb["b1t"], b1c)],
                  [(cwb, cwab), (sb["pmb"], pmat)])

        xs = DStream(p, "XD0", "XD1")
        od = DStream(p, "OD0", "OD1")
        oh = DStream(p, "OH0", "OH1")

        with nc.named_scope("L1_ln0"):
            xh_rdy = emit_ln_simple(p, sb, xs, x0)
        pe_stats = emit_ffn(p, sb, ws, xs, od, w0m,
                            [w1m[h] for h in range(NH)],
                            x0, x1_out, want_stats=True, tag="L1_ffn1",
                            xh_ready=xh_rdy)
        with nc.named_scope("L1_rows1"):
            ac_rows = emit_ln_rows(p, sb, pe_stats)

        ve1 = p.cnt["VE"]
        xb1 = xs.n
        xdi1 = p.cnt["XDI"]

        @blk.sync
        def _(sync):
            p.wait(sync, "PE", pe_stats)
            p.seq_dma("XDI", sync, wiqt, wiqm)
            p.seq_dma("XDI", sync, wikt, wikm)
            p.seq_dma("XDI", sync, sb["cosT"], cost)
            p.seq_dma("XDI", sync, sb["sinT"], sint)
            for k in range(NK):
                if k >= 2:
                    p.wait(sync, "VE", ve1 + 2 * (k - 1))
                else:
                    p.wait(sync, "VE", ve1)
                xs.dma(sync, sb["XS"][k % 2][:], x1_out[k])

        ac1b = p.cnt["AC"]
        fscr = sb["fscr"]

        @blk.vector
        def _(v):
            p.wait(v, "AC", ac_rows)
            od.wait_all(v)
            fscr = sb["fscr"]
            for k in range(NK):
                xs.wait_chunk(v, xb1 + k)
                if k >= 2:
                    p.wait(v, "AC", ac1b + (k - 1))
                v.tensor_tensor(fscr[k % 2][:], sb["XS"][k % 2][:],
                                sb["Rt"][:, :], ALU.mult)
                v.drain()
                v.tensor_tensor(fscr[k % 2][:], fscr[k % 2][:],
                                sb["Bt"][:, :], ALU.add)
                v.drain()
                inst = v.tensor_copy(xhf[:, 512 * k:512 * (k + 1)],
                                     fscr[k % 2][:, 0:512])
                p.inc("VE", inst, 2)

        @blk.scalar
        def _(s):
            for k in range(NK):
                p.wait(s, "VE", ve1 + 2 * (k + 1))
                inst = s.activation(sb["XH"][:, T * k:T * (k + 1)],
                                    fscr[k % 2][:], AF.Copy, bias=0.0,
                                    scale=1.0, accum_out=xsum[:, k:k + 1])
                p.inc("AC", inst)
                s.drain()
                oh.dma(s, xh1_out[k], sb["XH"][:, T * k:T * (k + 1)])

        XSb = [sb["XS"][i].bitcast(BF16) for i in range(2)]
        ktmp = [XSb[idx // 4][:, 512 * (idx % 4):512 * (idx % 4 + 1)]
                for idx in range(8)]
        with nc.named_scope("L1_kv"):
            emit_kproj_chunked(p, sb, ws, wkvm, ktmp,
                               lambda kc: [("AC", ac1b + kc + 1)])

        @blk.tensor
        def _(t):
            p.wait(t, "VE")
            p.wait(t, "AC")
            p.wait(t, "XDI", xdi1 + 64)
            for k in range(NK):
                t.matmul(PS[4][:, 0:1], wikt[:, 128 * k:128 * (k + 1)],
                         xsum[:, k:k + 1], start=(k == 0), stop=(k == NK - 1))
            for k in range(NK):
                mm = t.matmul(PS[5][:, :], wiqt[:, 128 * k:128 * (k + 1)],
                              xhf[:, 512 * k:512 * (k + 1)],
                              start=(k == 0), stop=(k == NK - 1))
            p.inc("PE", mm)
        pe_kiqi = p.cnt["PE"]

        @blk.scalar
        def _(s):
            p.wait(s, "PE", pe_kiqi)
            s.copy(kiT[:], PS[4][:, 0:1])
            inst = s.copy(qiT[:], PS[5][:, :])
            p.inc("AC", inst)
            s.drain()
            p.seq_dma("ODM", s, ki_out, kiT[:])
            p.seq_dma("ODM", s, qi_out, qiT[:])

        with nc.named_scope("L1_krope"):
            emit_krope(p, sb, ktmp, krot)

        ve_all = p.cnt["VE"]
        acC0 = p.cnt["AC"]
        peC0 = p.cnt["PE"]

        @blk.tensor
        def _(t):
            p.wait(t, "VE", ve_all)
            p.wait(t, "AC", acC0)
            p.wait(t, "GD")
            for idx in range(2 * NKV):
                i, x = divmod(idx, 2)
                if idx >= 1:
                    p.wait(t, "AC", acC0 + 2 * idx)
                sl = slice(i * T + 512 * x, i * T + 512 * (x + 1))
                t.matmul(PS[4][0:1, :], cwb[:, 0:1], krot[:, sl],
                         start=True, stop=True)
                mm = t.matmul(PS[5][0:1, :], cwb[:, 1:2], krot[:, sl],
                              start=True, stop=True)
                p.inc("PE", mm)

        @blk.scalar
        def _(s):
            for idx in range(2 * NKV):
                i, x = divmod(idx, 2)
                p.wait(s, "PE", peC0 + idx + 1)
                sl = slice(i * T + 512 * x, i * T + 512 * (x + 1))
                s.activation(eaT[:, sl], PS[4][0:1, :], AF.Exp)
                inst = s.activation(ebT[:, sl], PS[5][0:1, :], AF.Exp)
                p.inc("AC", inst, 2)
        ac_exp = p.cnt["AC"]

        @blk.vector
        def _(v):
            p.wait(v, "AC", ac_exp)
            for (src, dst) in ((eaT, za), (ebT, zb)):
                v.tensor_tensor(ztr[0][:, 0:2048], src[:, 0::2], src[:, 1::2],
                                ALU.add)
                v.drain()
                v.tensor_tensor(ztr[0][:, 2048:3072], ztr[0][:, 0:2048:2],
                                ztr[0][:, 1:2048:2], ALU.add)
                v.drain()
                v.tensor_tensor(ztr[0][:, 3072:3584], ztr[0][:, 2048:3072:2],
                                ztr[0][:, 2049:3072:2], ALU.add)
                v.drain()
                v.tensor_tensor(zsc[:], ztr[0][:, 3072:3584:2],
                                ztr[0][:, 3073:3584:2], ALU.add)
                v.drain()
                v.reciprocal(dst[:], zsc[:])
                v.drain()
            for r in range(16):
                v.tensor_tensor(wrow[:, r::16], eaT[:, r::16], za[:], ALU.mult)
            v.drain()
            for r in range(16):
                v.tensor_tensor(eaT[:, r::16], ebT[:, r::16], zb[:], ALU.mult)
            v.drain()
            v.tensor_tensor(wrow[:], wrow[:], eaT[:], ALU.add)
            v.drain()
            inst = v.tensor_scalar(wrow[:], wrow[:], 0.5, None, ALU.mult)
            v.drain()
            p.inc("VE", inst)
        ve_wrow = p.cnt["VE"]


        emit_proj_rope(p, sb, ws, wvm, NKV, vreg, False)

        peW0 = p.cnt["PE"]
        veW0 = p.cnt["VE"]

        @blk.tensor
        def _(t):
            p.wait(t, "VE", ve_wrow)
            for idx in range(2 * NKV):
                i, x = divmod(idx, 2)
                if idx >= 2:
                    p.wait(t, "VE", veW0 + (idx - 1))
                sl = slice(i * T + 512 * x, i * T + 512 * (x + 1))
                mm = t.matmul(PS[6 + (idx % 2)][:, :], sb["onesPf"][:],
                              wrow[:, sl], start=True, stop=True)
                p.inc("PE", mm)

        def tree_sum(eng, f, seg):
            eng.tensor_tensor(f[:, 512:768], f[:, 0:512:2], f[:, 1:512:2],
                              ALU.add)
            eng.drain()
            eng.tensor_tensor(f[:, 768:896], f[:, 512:768:2], f[:, 513:768:2],
                              ALU.add)
            eng.drain()
            eng.tensor_tensor(f[:, 896:960], f[:, 768:896:2], f[:, 769:896:2],
                              ALU.add)
            eng.drain()
            eng.tensor_tensor(seg[:, :], f[:, 896:960:2], f[:, 897:960:2],
                              ALU.add)

        ve_kw = [None] * (2 * NKV)
        gp0 = p.cnt["GP"]

        @blk.vector
        def _(v):
            fscr = sb["fscr"]
            p.wait(v, "AC")
            for idx in range(2 * NKV):
                i, x = divmod(idx, 2)
                sl = slice(i * T + 512 * x, i * T + 512 * (x + 1))
                p.wait(v, "PE", peW0 + idx + 1)
                if idx >= 1:
                    p.wait(v, "GP", gp0 + idx)
                v.drain()
                v.tensor_tensor(fscr[0][:, 0:512], krot[:, sl],
                                PS[6 + (idx % 2)][:, :], ALU.mult)
                inst = v.tensor_tensor(fscr[1][:, 0:512], vreg[:, sl],
                                       PS[6 + (idx % 2)][:, :], ALU.mult)
                ve_kw[idx] = p.inc("VE", inst)
                v.drain()
                tree_sum(v, fscr[0],
                         cks[:, 64 * i + 32 * x:64 * i + 32 * (x + 1)])
                v.drain()
            inst = v.tensor_copy(ckb[:], cks[:])
            p.inc("VE", inst)
        ve_done = p.cnt["VE"]

        @blk.gpsimd
        def _(g):
            for idx in range(2 * NKV):
                p.wait(g, "VE", ve_kw[idx])
                i, x = divmod(idx, 2)
                tree_sum(g, sb["fscr"][1],
                         cvs[:, 64 * i + 32 * x:64 * i + 32 * (x + 1)])
                g.drain()
                inst = g.engine_nop()
                p.inc("GP", inst, 1)
            g.drain()
            inst = g.tensor_copy(cvb[:], cvs[:])
            g.drain()
            inst = g.engine_nop()
            p.inc("GP", inst, 1)
        gp_done = p.cnt["GP"]

        @blk.scalar
        def _(s):
            p.wait(s, "VE", ve_done)
            p.wait(s, "GP", gp_done)
            p.seq_dma("ODM", s, ck_out, ckb[:])
            p.seq_dma("ODM", s, cv_out, cvb[:])

        @blk.sync
        def _(sync):
            od.wait_all(sync)
            oh.wait_all(sync)
            p.wait(sync, "ODM")

    return nc


def build_launch2():
    nc = bass.Bass("TRN2", target_bir_lowering=False, debug=False)
    f = lambda n, s, dt=F32: nc.dram_tensor(n, s, dt, kind="ExternalInput").ap()

    x1 = f("x1", [NH, 128, T])
    xh1 = f("xh1", [NK, 128, T], BF16)
    cktg = f("cktg", [NKV, 4, 128, 128], BF16)
    cvt = f("cvt", [NKV, 4, 128, 128], BF16)
    ltc = f("ltc", [4, 128, 512])
    iota64 = f("iota64", [128, 64])
    kis = f("kis", [128, NCORE])
    qi0 = f("qi0", [128, 512])
    wqm = f("wqm", [NH, 128, 2048])
    wom = f("wom", [NH, 128, 2048])
    w0m = f("w0m", [NJ, 128, 2048])
    w1m = f("w1m", [NH, 4, 128, 2048])
    b0c = f("b0c", [128, NJ])
    b1c = f("b1c", [128, NH])
    cost = f("cost", [128, T])
    sint = f("sint", [128, T])
    pmat = f("pmat", [128, 128])

    y_out = nc.dram_tensor("y_out", [NH, 128, T], F32,
                           kind="ExternalOutput").ap()
    x2scr = nc.dram_tensor("x2scr", [NH, 128, T], F32, kind="Internal").ap()

    sb = make_sbufs(nc)
    vf = sb["vf"]
    Dg = sb["Dreg"]
    qrot = Dg[:, 0:NH * T]
    attn = Dg[:, NH * T:2 * NH * T]
    cktt = Dg[:, 32768:34816]
    cvtt = Dg[:, 34816:36864]
    ebuf = [Dg[0:64, 36864 + 512 * i:36864 + 512 * (i + 1)] for i in range(2)]
    ck64 = Dg[:, 37888:38144]
    cv64 = Dg[0:64, 38144:38656]
    Pt = Dg[:, 38656:38912]
    LTt = Dg[:, 38912:40960]
    Bsb = vf[:, 21504:22016]
    cgt = vf[:, 22016:22528]
    Rsb = [vf[:, 22528:23040], vf[:, 23040:23552]]
    srow = vf[0:1, 23552:24064]
    denr = [vf[0:1, 24064:24576], vf[0:1, 24576:25088]]
    qi0t = vf[:, 25088:25600]
    kist = vf[:, 25600:25600 + NCORE]
    kisum = vf[:, 25632:25633]
    scol = vf[:, 25664:25668]
    cnt_ = vf[:, 25696:25700]
    maskc = vf[:, 25728:25732]
    mkb = Dg[:, 51520:51524]
    posc = vf[:, 25792:25796]
    iotat = vf[:, 25824:25888]

    with contextlib.ExitStack() as stack:
        blk = stack.enter_context(nc.Block())
        p = Prog(nc, stack, blk)
        ws = WStream(p, sb["wbuf"])
        PS = sb["PS"]

        xs = DStream(p, "XD0", "XD1")
        od = DStream(p, "OD0", "OD1")

        emit_init(p, sb,
                  [(sb["b0t"], b0c), (sb["b1t"], b1c), (sb["cosT"], cost),
                   (sb["sinT"], sint), (qi0t, qi0), (kist, kis)],
                  [(sb["pmb"], pmat)])

        @blk.sync
        def _(sync):
            p.seq_dma("XDI", sync, iotat, iota64)
            for k in range(NK):
                p.seq_dma("XDI", sync, sb["XH"][:, T * k:T * (k + 1)], xh1[k])
            for i in range(NKV):
                for c in range(4):
                    p.seq_dma(
                        "XDI", sync,
                        cktt[:, 512 * i + 128 * c:512 * i + 128 * (c + 1)],
                        cktg[i, c])
                    p.seq_dma(
                        "XDI", sync,
                        cvtt[:, 512 * i + 128 * c:512 * i + 128 * (c + 1)],
                        cvt[i, c])
        xd_data = p.cnt["XDI"]

        @blk.gpsimd
        def _(g):
            for jc in range(4):
                p.seq_dma("GD", g, LTt[:, 512 * jc:512 * (jc + 1)], ltc[jc])
        gd_lt = p.cnt["GD"]

        @blk.tensor
        def _(t):
            p.wait(t, "XDI", xd_data)

        with nc.named_scope("L2_qproj"):
            emit_proj_rope(p, sb, ws, wqm, NH, qrot, True)
        ve_q = p.cnt["VE"]

        @blk.vector
        def _(v):
            p.wait(v, "XDI", xd_data)
            inst = v.reduce_sum(kisum[:], kist[:], axis=AX)
            p.inc("VE", inst)
        ve_ki = p.cnt["VE"]

        @blk.tensor
        def _(t):
            p.wait(t, "VE", ve_ki)
            p.wait(t, "VE", ve_q)
            t.matmul(PS[4][0:1, :], kisum[:], qi0t[:], start=True, stop=True)
            for c in range(4):
                mm = t.matmul(PS[5][:, c:c + 1],
                              qi0t[:, 128 * c:128 * (c + 1)], kisum[:],
                              start=True, stop=True)
            p.inc("PE", mm)
        pe_s1 = p.cnt["PE"]

        @blk.scalar
        def _(s):
            p.wait(s, "PE", pe_s1)
            s.copy(srow[:], PS[4][0:1, :])
            inst = s.copy(scol[:], PS[5][:, 0:4])
            p.inc("AC", inst)
        ac_s1 = p.cnt["AC"]

        @blk.tensor
        def _(t):
            p.wait(t, "AC", ac_s1)
            mm = t.matmul(PS[6][:, :], sb["onesPf"][:], srow[:],
                          start=True, stop=True)
            p.inc("PE", mm)
        pe_s2 = p.cnt["PE"]

        @blk.scalar
        def _(s):
            p.wait(s, "PE", pe_s2)
            inst = s.copy(Bsb[:], PS[6][:, :])
            p.inc("AC", inst)
        ac_s2 = p.cnt["AC"]

        @blk.vector
        def _(v):
            p.wait(v, "AC", ac_s2)
            for c in range(4):
                v.tensor_scalar(cgt[:], Bsb[:], scol[:, c:c + 1], None,
                                ALU.is_gt)
                v.drain()
                v.reduce_sum(cnt_[:, c:c + 1], cgt[:], axis=AX)
                v.drain()
            v.tensor_single_scalar(maskc[:], cnt_[:], 63.5, ALU.is_lt)
            v.drain()
            inst = v.tensor_copy(mkb[:], maskc[:])
            v.drain()
            p.inc("VE", inst)
        ve_mask = p.cnt["VE"]

        @blk.tensor
        def _(t):
            p.wait(t, "VE", ve_mask)
            p.wait(t, "GD", gd_lt)
            for nc_ in range(4):
                for jc in range(nc_ + 1):
                    mm = t.matmul(PS[7][:, nc_:nc_ + 1],
                                  LTt[:, 512 * jc + 128 * nc_:
                                      512 * jc + 128 * (nc_ + 1)],
                                  mkb[:, jc:jc + 1],
                                  start=(jc == 0), stop=(jc == nc_))
            p.inc("PE", mm)
        pe_pos = p.cnt["PE"]

        @blk.scalar
        def _(s):
            p.wait(s, "PE", pe_pos)
            inst = s.copy(posc[:], PS[7][:, 0:4])
            p.inc("AC", inst)
        ac_pos = p.cnt["AC"]

        @blk.vector
        def _(v):
            p.wait(v, "AC", ac_pos)
            for c in range(4):
                v.tensor_scalar(cgt[:, 0:64], iotat[:], posc[:, c:c + 1],
                                None, ALU.is_equal)
                v.drain()
                v.tensor_scalar(Pt[:, 64 * c:64 * (c + 1)], cgt[:, 0:64],
                                maskc[:, c:c + 1], None, ALU.mult)
                v.drain()
            inst = v.engine_nop()
            p.inc("VE", inst)
        ve_P = p.cnt["VE"]

        ac_g0 = p.cnt["AC"]

        @blk.tensor
        def _(t):
            p.wait(t, "VE", ve_P)
            for i in range(NKV):
                if i >= 1:
                    p.wait(t, "AC", ac_g0 + i)
                for c in range(4):
                    sl = slice(512 * i + 128 * c, 512 * i + 128 * (c + 1))
                    t.matmul(PS[2][:, 0:64], cktt[:, sl],
                             Pt[:, 64 * c:64 * (c + 1)],
                             start=(c == 0), stop=(c == 3))
                    mm = t.matmul(PS[3][0:64, 0:128],
                                  Pt[:, 64 * c:64 * (c + 1)],
                                  cvtt[:, sl], start=(c == 0), stop=(c == 3))
                p.inc("PE", mm)
        pe_g0 = p.cnt["PE"] - NKV

        @blk.scalar
        def _(s):
            for i in range(NKV):
                p.wait(s, "PE", pe_g0 + i + 1)
                s.copy(ck64[:, 64 * i:64 * (i + 1)], PS[2][:, 0:64])
                inst = s.copy(cv64[:, 128 * i:128 * (i + 1)],
                              PS[3][0:64, 0:128])
                p.inc("AC", inst)
        @blk.tensor
        def _(t):
            pass
        ac_g = p.cnt["AC"]

        NIT = 2 * NH
        peA0 = p.cnt["PE"]
        acA0 = p.cnt["AC"]
        veA0 = p.cnt["VE"]

        def sched(pairs):
            out = []
            for s_ in range(NIT + 3):
                for kind, lag in pairs:
                    it = s_ - lag
                    if 0 <= it < NIT:
                        out.append((kind, it))
            return out

        pe_sched = sched([("lg", 0), ("denav", 1), ("R", 2)])
        ac_sched = sched([("exp", 0), ("Rc", 2)])
        ve_sched = sched([("out", 3), ("rc", 1)])
        pe_at, ac_at, ve_at = {}, {}, {}
        c = peA0
        for ev in pe_sched:
            c += 1
            pe_at[ev] = c
        c = acA0
        for ev in ac_sched:
            c += 1
            ac_at[ev] = c
        c = veA0
        for ev in ve_sched:
            c += 1
            ve_at[ev] = c

        def itsl(it):
            h, x = divmod(it, 2)
            return h // 4, slice(T * h + 512 * x, T * h + 512 * (x + 1))

        @blk.tensor
        def _(t):
            p.wait(t, "AC", ac_g)
            for kind, it in pe_sched:
                pr = it % 2
                hk, qsl = itsl(it)
                if kind == "lg":
                    if it >= 2:
                        p.wait(t, "AC", ac_at[("exp", it - 2)])
                    mm = t.matmul(PS[pr][0:64, :],
                                  ck64[:, 64 * hk:64 * (hk + 1)],
                                  qrot[:, qsl], start=True, stop=True)
                    p.inc("PE", mm)
                elif kind == "denav":
                    p.wait(t, "AC", ac_at[("exp", it)])
                    if it >= 2:
                        p.wait(t, "VE", ve_at[("out", it - 2)])
                    t.matmul(PS[4 + pr][0:1, :], sb["onesb"][0:64, :],
                             ebuf[pr][:], start=True, stop=True)
                    mm = t.matmul(PS[2 + pr][:, :],
                                  cv64[:, 128 * hk:128 * (hk + 1)],
                                  ebuf[pr][:], start=True, stop=True)
                    p.inc("PE", mm)
                else:
                    p.wait(t, "VE", ve_at[("rc", it)])
                    mm = t.matmul(PS[6 + pr][:, :], sb["onesPf"][:],
                                  denr[pr][:], start=True, stop=True)
                    p.inc("PE", mm)

        @blk.scalar
        def _(s):
            for kind, it in ac_sched:
                pr = it % 2
                if kind == "exp":
                    p.wait(s, "PE", pe_at[("lg", it)])
                    inst = s.activation(ebuf[pr][:], PS[pr][0:64, :], AF.Exp,
                                        bias=0.0, scale=ISQ)
                    p.inc("AC", inst)
                else:
                    p.wait(s, "PE", pe_at[("R", it)])
                    if it >= 2:
                        p.wait(s, "VE", ve_at[("out", it - 2)])
                    inst = s.copy(Rsb[pr][:], PS[6 + pr][:, :])
                    p.inc("AC", inst)

        @blk.vector
        def _(v):
            for kind, it in ve_sched:
                pr = it % 2
                hk, qsl = itsl(it)
                if kind == "rc":
                    p.wait(v, "PE", pe_at[("denav", it)])
                    inst = v.reciprocal(denr[pr][:], PS[4 + pr][0:1, :])
                    p.inc("VE", inst)
                else:
                    p.wait(v, "AC", ac_at[("Rc", it)])
                    inst = v.tensor_tensor(attn[:, qsl], PS[2 + pr][:, :],
                                           Rsb[pr][:], ALU.mult)
                    p.inc("VE", inst)

        scope_wo = nc.named_scope("L2_wo")
        scope_wo.__enter__()
        ws.load_run(wom, NH)
        ve_attn = p.cnt["VE"]
        wcW0 = ws.n_used
        pe_S = [None] * NH
        xbW = xs.n
        obW = od.n

        @blk.sync
        def _(sync):
            for h in range(NH):
                if h >= 2:
                    p.wait(sync, "VE", ve_attn + 3 * (h - 1))
                xs.dma(sync, sb["XS"][h % 2][:], x1[h])

        @blk.tensor
        def _(t):
            for h in range(NH):
                a, b = PS[2 * (h % 2)], PS[2 * (h % 2) + 1]
                if h == 0:
                    p.wait(t, "AC", ac_at[("exp", NIT - 2)])
                    p.wait(t, "AC", ac_at[("exp", NIT - 1)])
                elif h == 1:
                    p.wait(t, "VE", ve_at[("out", NIT - 1)])
                else:
                    p.wait(t, "VE", ve_attn + 3 * (h - 1))
                w = ws.acquire(t)
                for k in range(NK):
                    if h == 0:
                        p.wait(t, "VE", ve_at[("out", 2 * k + 1)])
                    t.matmul(a[:], w[:, 128 * k:128 * (k + 1)],
                             attn[:, T * k:T * k + 512],
                             start=(k == 0), stop=(k == NK - 1))
                    mm = t.matmul(b[:], w[:, 128 * k:128 * (k + 1)],
                                  attn[:, T * k + 512:T * (k + 1)],
                                  start=(k == 0), stop=(k == NK - 1))
                ws.release(mm)
                if h >= 1:
                    p.wait(t, "VE", ve_attn + 3 * h)
                    mm = emit_stats_mms(p, t, PS, sb["onesb"],
                                        sb["XH"][:, T * (h - 1):T * h],
                                        sb["xsq"][(h - 1) % 2],
                                        h - 1 == 0, False)
                    pe_S[h - 1] = p.inc("PE", mm)
            p.wait(t, "VE", ve_attn + 3 * NH)
            mm = emit_stats_mms(p, t, PS, sb["onesb"],
                                sb["XH"][:, T * (NH - 1):T * NH],
                                sb["xsq"][(NH - 1) % 2], False, True)
            pe_S[NH - 1] = p.inc("PE", mm)

        @blk.vector
        def _(v):
            for h in range(NH):
                p.wait(v, "WC", wcW0 + h + 1)
                xs.wait_chunk(v, xbW + h)
                if h >= 2:
                    od.wait_chunk(v, obW + h - 2)
                    p.wait(v, "PE", pe_S[h - 2])
                a, b = PS[2 * (h % 2)], PS[2 * (h % 2) + 1]
                v.tensor_tensor(sb["x1t"][h % 2][:, 0:512], a[:],
                                sb["XS"][h % 2][:, 0:512], ALU.add)
                v.tensor_tensor(sb["x1t"][h % 2][:, 512:1024], b[:],
                                sb["XS"][h % 2][:, 512:1024], ALU.add)
                v.drain()
                v.tensor_copy(sb["XH"][:, T * h:T * (h + 1)],
                              sb["x1t"][h % 2][:])
                v.drain()
                inst = v.tensor_tensor(sb["xsq"][h % 2][:],
                                       sb["x1t"][h % 2][:],
                                       sb["x1t"][h % 2][:], ALU.mult)
                p.inc("VE", inst, 3)

        @blk.scalar
        def _(s):
            for h in range(NH):
                p.wait(s, "VE", ve_attn + 3 * (h + 1))
                od.dma(s, x2scr[h], sb["x1t"][h % 2][:])

        scope_wo.__exit__(None, None, None)
        with nc.named_scope("L2_rows2"):
            ac_rows2 = emit_ln_rows(p, sb, pe_S[NH - 1])

        ve2 = p.cnt["VE"]
        gp2b = p.cnt["GP"]
        xb2 = xs.n

        @blk.sync
        def _(sync):
            for k in range(NK):
                if k >= 2:
                    p.wait(sync, "VE", ve2 + 2 * (k - 1))
                    p.wait(sync, "GP", gp2b + (k - 1))
                else:
                    p.wait(sync, "VE", ve2)
                xs.dma(sync, sb["XS"][k % 2][:], x2scr[k])

        SPL = 384

        @blk.vector
        def _(v):
            p.wait(v, "AC", ac_rows2)
            od.wait_all(v)
            fscr = sb["fscr"]
            for k in range(NK):
                xs.wait_chunk(v, xb2 + k)
                if k >= 2:
                    p.wait(v, "GP", gp2b + (k - 1))
                inst = v.tensor_tensor(fscr[k % 2][:], sb["XS"][k % 2][:],
                                       sb["Rt"][:, :], ALU.mult)
                p.inc("VE", inst)
                v.drain()
                inst = v.tensor_tensor(sb["XH"][:, T * k:T * k + SPL],
                                       fscr[k % 2][:, 0:SPL],
                                       sb["Bt"][:, 0:SPL], ALU.add)
                p.inc("VE", inst)

        @blk.gpsimd
        def _(g):
            fscr = sb["fscr"]
            for k in range(NK):
                p.wait(g, "VE", ve2 + 2 * k + 1)
                inst = g.tensor_tensor(sb["XH"][:, T * k + SPL:T * (k + 1)],
                                       fscr[k % 2][:, SPL:1024],
                                       sb["Bt"][:, SPL:1024], ALU.add)
                g.drain()
                inst = g.engine_nop()
                p.inc("GP", inst)

        xh2_rdy = (lambda v2, g2: (lambda k: [("VE", v2 + 2 * (k + 1)),
                                              ("GP", g2 + (k + 1))]))(ve2, gp2b)
        emit_ffn(p, sb, ws, xs, od, w0m, [w1m[h] for h in range(NH)],
                 x2scr, y_out, want_stats=False, tag="L2_ffn3",
                 xh_ready=xh2_rdy)

        @blk.sync
        def _(sync):
            od.wait_all(sync)

    return nc


def _rope_tables(core):
    half = 64
    t = np.arange(core * T, (core + 1) * T, dtype=np.float64)
    freqs = (1.0 / 10000.0) ** (np.arange(half, dtype=np.float64) / half)
    ang = np.outer(t, freqs)
    cosv, sinv = np.cos(ang), np.sin(ang)
    cosT = np.empty((128, T), np.float32)
    sinT = np.empty((128, T), np.float32)
    cosT[0::2, :] = cosv.T
    cosT[1::2, :] = cosv.T
    sinT[0::2, :] = -sinv.T
    sinT[1::2, :] = sinv.T
    return cosT, sinT


def _layouts(inputs):
    g = {}
    x = np.asarray(inputs["L0_attn_x"])[0]
    g["x0"] = np.ascontiguousarray(
        x.reshape(NCORE, T, NK, 128).transpose(0, 2, 3, 1))
    W0 = np.asarray(inputs["L0_ffn_W0"])
    g["w0m"] = np.ascontiguousarray(
        W0.reshape(NJ, 128, NK, 128).transpose(0, 3, 2, 1)).reshape(
            NJ, 128, 2048)
    W1 = np.asarray(inputs["L0_ffn_W1"])
    g["w1m"] = np.ascontiguousarray(
        W1.reshape(NH, 128, 4, NK, 128).transpose(0, 2, 4, 3, 1)).reshape(
            NH, 4, 128, 2048)
    g["b0c"] = np.ascontiguousarray(
        np.asarray(inputs["L0_ffn_bias0"]).reshape(NJ, 128).T)
    g["b1c"] = np.ascontiguousarray(
        np.asarray(inputs["L0_ffn_bias1"]).reshape(NH, 128).T)

    def proj(w, nout):
        return np.ascontiguousarray(
            np.asarray(w).reshape(NK, 128, nout, 128).transpose(2, 1, 0, 3)
        ).reshape(nout, 128, 2048)

    g["wqm"] = proj(inputs["L0_attn_wq"], NH)
    g["wom"] = proj(inputs["L0_attn_wo"], NH)
    wk = np.asarray(inputs["L0_attn_wk"])
    g["wkvm"] = np.ascontiguousarray(
        wk.reshape(4, 4, 128, 4, 128).transpose(0, 2, 1, 3, 4)).reshape(
            4, 128, 2048)
    g["wvm"] = proj(inputs["L0_attn_wv"], NKV)
    g["wiqm"] = proj(inputs["L0_attn_wiq"], 1).reshape(128, 2048)
    g["wikm"] = proj(inputs["L0_attn_wik"], 1).reshape(128, 2048)
    g["cwab"] = np.ascontiguousarray(
        np.stack([np.asarray(inputs["L0_attn_cwa"]),
                  np.asarray(inputs["L0_attn_cwb"])], axis=1)).astype(
                      np.float32)
    pm = np.zeros((128, 128), np.float32)
    for i in range(128):
        pm[i, i ^ 1] = 1.0
    g["pmat"] = pm
    return g


def kernel(**inputs):
    g = _layouts(inputs)
    tabs = [_rope_tables(c) for c in range(NCORE)]

    nc1 = build_launch1()
    shared1 = {k: g[k] for k in ["w0m", "w1m", "b0c", "b1c", "wkvm", "wvm",
                                 "wiqm", "wikm", "cwab", "pmat"]}
    in1 = [dict(shared1, x0=g["x0"][c], cost=tabs[c][0], sint=tabs[c][1])
           for c in range(NCORE)]
    r1 = run_bass_kernel_spmd(nc1, in1, core_ids=list(range(NCORE))).results

    ck = np.stack([r["ck_out"] for r in r1])
    cv = np.stack([r["cv_out"] for r in r1])

    def nmajor(a):
        t = a.reshape(NCORE, 128, NKV, 64).transpose(2, 0, 3, 1)
        return np.ascontiguousarray(t).reshape(NKV, 4, 128, 128)

    cktg = nmajor(ck)
    cvt = nmajor(cv)
    kis = np.concatenate([r["ki_out"] for r in r1], axis=1)
    qi0 = r1[0]["qi_out"]

    nc2 = build_launch2()
    shared2 = {k: g[k] for k in ["wqm", "wom", "w0m", "w1m", "b0c", "b1c",
                                 "pmat"]}
    ltc = (np.arange(512)[:, None] < np.arange(512)[None, :]).astype(
        np.float32).reshape(4, 128, 512)
    iota = np.tile(np.arange(64, dtype=np.float32), (128, 1))
    shared2.update(cktg=cktg, cvt=cvt, kis=kis, qi0=qi0, ltc=ltc, iota64=iota)
    in2 = [dict(shared2, x1=r1[c]["x1_out"], xh1=r1[c]["xh1_out"],
                cost=tabs[c][0], sint=tabs[c][1])
           for c in range(NCORE)]
    r2 = run_bass_kernel_spmd(nc2, in2, core_ids=list(range(NCORE))).results

    y = np.stack([r["y_out"] for r in r2])
    out = y.transpose(0, 3, 1, 2).reshape(1, NCORE * T, D)
    return np.ascontiguousarray(out.astype(np.float32))


# version 10
# speedup vs baseline: 1.0240x; 1.0240x over previous
import contextlib
import numpy as np
import ml_dtypes
import concourse.bass as bass
import concourse.mybir as mybir
from concourse.bass_utils import run_bass_kernel_spmd

F32 = mybir.dt.float32
BF16 = mybir.dt.bfloat16
AF = mybir.ActivationFunctionType
ALU = mybir.AluOpType
AX = mybir.AxisListType.X

NCORE = 8
T = 1024
D = 2048
NK = 16
NJ = 64
NH = 16
NKV = 4
EPS = 1e-5
MASK_NEG = 1.0e4
ISQ = 0.08838834764831845


class Prog:
    def __init__(self, nc, stack, blk):
        self.nc = nc
        self.blk = blk
        names = ["WD0", "WD1", "WD2", "WD3", "WC", "XD0", "XD1", "OD0",
                 "OD1", "OH0", "OH1", "XDI", "ODM", "PE", "AC", "VE", "GD",
                 "GP"]
        self.sem = {n: stack.enter_context(nc.semaphore(n)) for n in names}
        self.cnt = {n: 0 for n in names}

    def inc(self, name, inst, k=1):
        inst.then_inc(self.sem[name], k)
        self.cnt[name] += k
        return self.cnt[name]

    def wait(self, eng, name, val=None):
        v = self.cnt[name] if val is None else val
        if v > 0:
            eng.wait_ge(self.sem[name], v)

    def seq_dma(self, name, eng, dst, src):
        self.wait(eng, name)
        return self.inc(name, eng.dma_start(dst, src), 16)


class DStream:

    def __init__(self, p, s0, s1):
        self.p = p
        self.names = (s0, s1)
        self.n = 0

    def dma(self, eng, dst, src):
        i = self.n
        self.p.seq_dma(self.names[i % 2], eng, dst, src)
        self.n += 1
        return i

    def wait_chunk(self, eng, i):
        self.p.wait(eng, self.names[i % 2], 16 * (i // 2 + 1))

    def wait_all(self, eng):
        self.p.wait(eng, self.names[0])
        self.p.wait(eng, self.names[1])


class WStream:

    NBUF = 2

    def __init__(self, p, wbuf):
        self.p = p
        self.wbuf = wbuf
        self.n_dma = 0
        self.n_used = 0

    def load_run(self, parent, count):
        assert count % 2 == 0
        p = self.p

        @p.blk.gpsimd
        def _(g):
            for c2 in range(count // 2):
                i = self.n_dma
                P = i // 2
                if P >= self.NBUF:
                    p.wait(g, "WC", 2 * (P - self.NBUF) + 2)
                dst = self.wbuf[P % self.NBUF][:].rearrange(
                    "p (c t) -> p c t", c=2)
                srcp = parent[2 * c2:2 * c2 + 2].rearrange("c p t -> p c t")
                p.seq_dma("WD%d" % (P % self.NBUF), g, dst, srcp)
                self.n_dma += 2

    def acquire(self, eng):
        i = self.n_used
        P = i // 2
        self.p.wait(eng, "WD%d" % (P % self.NBUF), 16 * (P // self.NBUF + 1))
        return self.wbuf[P % self.NBUF][:, 2048 * (i % 2):2048 * (i % 2 + 1)]

    def release(self, inst):
        self.p.inc("WC", inst, 1)
        self.n_used += 1


def emit_stats_mms(p, t, PS, onesb, xb, xq, first, last):
    for x in range(2):
        sl = slice(512 * x, 512 * (x + 1))
        t.matmul(PS[4 + x][0:1, :], onesb[:], xb[:, sl], start=first, stop=last)
        mm = t.matmul(PS[6 + x][0:1, :], onesb[:], xq[:, sl],
                      start=first, stop=last)
    return mm


def emit_ln_rows(p, sb, pe_stats_done):
    blk, PS = p.blk, sb["PS"]
    rmean, rvar, rR = sb["rows"]

    @blk.vector
    def _(v):
        p.wait(v, "PE", pe_stats_done)
        for x in range(2):
            sl = slice(512 * x, 512 * (x + 1))
            v.tensor_scalar(rmean[:, sl], PS[4 + x][0:1, :], 1.0 / D, None,
                            ALU.mult)
            v.tensor_scalar(rvar[:, sl], PS[6 + x][0:1, :], 1.0 / D, None,
                            ALU.mult)
        v.drain()
        v.tensor_tensor(rR[:, :], rmean[:, :], rmean[:, :], ALU.mult)
        v.drain()
        inst = v.tensor_tensor(rvar[:, :], rvar[:, :], rR[:, :], ALU.subtract)
        p.inc("VE", inst)
    ve_var = p.cnt["VE"]

    @blk.scalar
    def _(s):
        p.wait(s, "VE", ve_var)
        s.activation(rR[:, :], rvar[:, :], AF.Ln, bias=sb["epst"][:], scale=1.0)
        s.drain()
        inst = s.activation(rR[:, :], rR[:, :], AF.Exp, bias=0.0, scale=-0.5)
        p.inc("AC", inst)
    ac_rstd = p.cnt["AC"]

    @blk.vector
    def _(v):
        p.wait(v, "AC", ac_rstd)
        inst = v.scalar_tensor_tensor(rmean[:, :], rmean[:, :], -1.0, rR[:, :],
                                      ALU.mult, ALU.mult)
        p.inc("VE", inst)
    ve_b = p.cnt["VE"]

    @blk.tensor
    def _(t):
        p.wait(t, "VE", ve_b)
        for x in range(2):
            sl = slice(512 * x, 512 * (x + 1))
            t.matmul(PS[4 + x][:, :], sb["onesPf"][:], rR[:, sl],
                     start=True, stop=True)
            mm = t.matmul(PS[6 + x][:, :], sb["onesPf"][:], rmean[:, sl],
                          start=True, stop=True)
        p.inc("PE", mm)
    pe_bc = p.cnt["PE"]

    @blk.scalar
    def _(s):
        p.wait(s, "PE", pe_bc)
        p.wait(s, "OD0")
        p.wait(s, "OD1")
        for x in range(2):
            sl = slice(512 * x, 512 * (x + 1))
            s.copy(sb["Rt"][:, sl], PS[4 + x][:, :])
            inst = s.copy(sb["Bt"][:, sl], PS[6 + x][:, :])
        p.inc("AC", inst)
    return p.cnt["AC"]


def emit_ln_simple(p, sb, xs, x_hbm):
    blk, PS, XH = p.blk, sb["PS"], sb["XH"]
    xsq, fscr = sb["xsq"], sb["fscr"]
    stg = sb["vf"][:, 0:NK * T]
    ve0, pe0 = p.cnt["VE"], p.cnt["PE"]
    gp0 = p.cnt["GP"]
    xb0 = xs.n

    def grp_dst(i):
        return stg[:, 4096 * i:4096 * (i + 1)].rearrange(
            "p (c t) -> p c t", c=4)

    def grp_src(i):
        return x_hbm[4 * i:4 * (i + 1)].rearrange("c p t -> p c t")

    @blk.sync
    def _(sync):
        xs.dma(sync, grp_dst(0), grp_src(0))

    @blk.gpsimd
    def _(g):
        p.seq_dma("WD3", g, grp_dst(1), grp_src(1))
        p.seq_dma("WD3", g, grp_dst(3), grp_src(3))

    @blk.scalar
    def _(s):
        p.seq_dma("WD2", s, grp_dst(2), grp_src(2))
    wd2_ln = p.cnt["WD2"]

    def grp_wait(eng, k):
        g = k // 4
        if g == 0:
            xs.wait_chunk(eng, xb0)
        elif g == 1:
            p.wait(eng, "WD3", 16)
        elif g == 2:
            p.wait(eng, "WD2", wd2_ln)
        else:
            p.wait(eng, "WD3", 32)

    def stgc(k):
        return stg[:, T * k:T * (k + 1)]

    @blk.vector
    def _(v):
        for k in range(NK):
            grp_wait(v, k)
            inst = v.tensor_copy(XH[:, T * k:T * (k + 1)], stgc(k))
            p.inc("VE", inst)

    @blk.gpsimd
    def _(g):
        for k in range(NK):
            grp_wait(g, k)
            if k >= 2:
                p.wait(g, "PE", pe0 + (k - 1))
            inst = g.tensor_tensor(xsq[k % 2][:], stgc(k), stgc(k), ALU.mult)
            g.drain()
            inst = g.engine_nop()
            p.inc("GP", inst)

    @blk.tensor
    def _(t):
        for k in range(NK):
            p.wait(t, "VE", ve0 + (k + 1))
            p.wait(t, "GP", gp0 + (k + 1))
            mm = emit_stats_mms(p, t, PS, sb["onesb"],
                                XH[:, T * k:T * (k + 1)], xsq[k % 2],
                                k == 0, k == NK - 1)
            p.inc("PE", mm)
    pe_stats = p.cnt["PE"]

    ac_rows = emit_ln_rows(p, sb, pe_stats)

    ve1 = p.cnt["VE"]
    gp1 = p.cnt["GP"]
    SPL = 384

    @blk.vector
    def _(v):
        p.wait(v, "AC", ac_rows)
        for k in range(NK):
            if k >= 2:
                p.wait(v, "GP", gp1 + (k - 1))
            inst = v.tensor_tensor(fscr[k % 2][:], stgc(k),
                                   sb["Rt"][:, :], ALU.mult)
            p.inc("VE", inst)
            v.drain()
            inst = v.tensor_tensor(XH[:, T * k:T * k + SPL],
                                   fscr[k % 2][:, 0:SPL],
                                   sb["Bt"][:, 0:SPL], ALU.add)
            p.inc("VE", inst)

    @blk.gpsimd
    def _(g):
        for k in range(NK):
            p.wait(g, "VE", ve1 + 2 * k + 1)
            inst = g.tensor_tensor(XH[:, T * k + SPL:T * (k + 1)],
                                   fscr[k % 2][:, SPL:1024],
                                   sb["Bt"][:, SPL:1024], ALU.add)
            g.drain()
            inst = g.engine_nop()
            p.inc("GP", inst)

    return lambda k: [("VE", ve1 + 2 * (k + 1)), ("GP", gp1 + (k + 1))]


def emit_ffn(p, sb, ws, xs, od, w0_chunks, w1_chunks, x_res_hbm, out_hbm,
             want_stats, tag="ffn", xh_ready=None):
    blk, PS, XS = p.blk, sb["PS"], sb["XS"]
    Dreg, XH = sb["Dreg"], sb["XH"]
    etmp, x1t = sb["etmp"], sb["x1t"]

    ws.load_run(w0_chunks, NJ)
    ve_xhat = p.cnt["VE"]
    ac0 = p.cnt["AC"]
    wcA0 = ws.n_used
    scope_a = p.nc.named_scope(tag + "_A")
    scope_a.__enter__()

    gp_xhat = p.cnt["GP"]

    @blk.tensor
    def _(t):
        if xh_ready is None:
            p.wait(t, "VE", ve_xhat)
            p.wait(t, "GP", gp_xhat)
        p.wait(t, "AC", ac0)
        for j in range(NJ):
            w = ws.acquire(t)
            if j >= 2:
                p.wait(t, "AC", ac0 + (j - 1))
            a, b = PS[2 * (j % 2)], PS[2 * (j % 2) + 1]
            for k in range(NK):
                if j == 0 and xh_ready is not None:
                    for semn, val in xh_ready(k):
                        p.wait(t, semn, val)
                t.matmul(a[:], w[:, 128 * k:128 * (k + 1)],
                         XH[:, T * k:T * k + 512],
                         start=(k == 0), stop=(k == NK - 1))
                mm = t.matmul(b[:], w[:, 128 * k:128 * (k + 1)],
                              XH[:, T * k + 512:T * (k + 1)],
                              start=(k == 0), stop=(k == NK - 1))
            ws.release(mm)

    @blk.scalar
    def _(s):
        for j in range(NJ):
            p.wait(s, "WC", wcA0 + j + 1)
            a, b = PS[2 * (j % 2)], PS[2 * (j % 2) + 1]
            s.activation(Dreg[:, T * j:T * j + 512], a[:], AF.Gelu,
                         bias=sb["b0t"][:, j:j + 1], scale=1.0)
            inst = s.activation(Dreg[:, T * j + 512:T * (j + 1)], b[:], AF.Gelu,
                                bias=sb["b0t"][:, j:j + 1], scale=1.0)
            p.inc("AC", inst)
    ac_phA = p.cnt["AC"]
    scope_a.__exit__(None, None, None)
    scope_b = p.nc.named_scope(tag + "_B")
    scope_b.__enter__()

    for w1h in w1_chunks:
        ws.load_run(w1h, 4)
    veB = p.cnt["VE"]
    wcB0 = ws.n_used
    vper = 3 if want_stats else 1
    pe_S = [None] * NH
    xbB = xs.n
    obB = od.n

    @blk.sync
    def _(sync):
        for h in range(NH):
            if h >= 2:
                p.wait(sync, "VE", veB + vper * (h - 1))
            else:
                p.wait(sync, "VE", veB)
            xs.dma(sync, XS[h % 2][:], x_res_hbm[h])

    @blk.tensor
    def _(t):
        p.wait(t, "AC", ac_phA)
        for h in range(NH):
            a, b = PS[2 * (h % 2)], PS[2 * (h % 2) + 1]
            if h >= 2:
                p.wait(t, "AC", ac_phA + (h - 1))
            for jg in range(4):
                w = ws.acquire(t)
                for k in range(NK):
                    j = 16 * jg + k
                    st, sp = (j == 0), (j == NJ - 1)
                    t.matmul(a[:], w[:, 128 * k:128 * (k + 1)],
                             Dreg[:, T * j:T * j + 512], start=st, stop=sp)
                    mm = t.matmul(b[:], w[:, 128 * k:128 * (k + 1)],
                                  Dreg[:, T * j + 512:T * (j + 1)],
                                  start=st, stop=sp)
                ws.release(mm)
            if want_stats and h >= 1:
                p.wait(t, "VE", veB + vper * h)
                mm = emit_stats_mms(p, t, PS, sb["onesb"],
                                    XH[:, T * (h - 1):T * h],
                                    sb["xsq"][(h - 1) % 2],
                                    h - 1 == 0, False)
                pe_S[h - 1] = p.inc("PE", mm)
        if want_stats:
            p.wait(t, "VE", veB + vper * NH)
            mm = emit_stats_mms(p, t, PS, sb["onesb"],
                                XH[:, T * (NH - 1):T * NH],
                                sb["xsq"][(NH - 1) % 2], False, True)
            pe_S[NH - 1] = p.inc("PE", mm)

    @blk.scalar
    def _(s):
        for h in range(NH):
            if h >= 2:
                p.wait(s, "VE", veB + vper * (h - 2) + 1)
                od.dma(s, out_hbm[h - 2], x1t[(h - 2) % 2][:])
            p.wait(s, "WC", wcB0 + 4 * (h + 1))
            if h >= 2:
                p.wait(s, "VE", veB + vper * (h - 1))
            a, b = PS[2 * (h % 2)], PS[2 * (h % 2) + 1]
            et = etmp[h % 2]
            s.activation(et[:, 0:512], a[:], AF.Copy, bias=0.0, scale=0.5)
            inst = s.activation(et[:, 512:1024], b[:], AF.Copy, bias=0.0,
                                scale=0.5)
            p.inc("AC", inst)
        for h in (NH - 2, NH - 1):
            p.wait(s, "VE", veB + vper * h + 1)
            od.dma(s, out_hbm[h], x1t[h % 2][:])

    @blk.vector
    def _(v):
        for h in range(NH):
            p.wait(v, "AC", ac_phA + (h + 1))
            xs.wait_chunk(v, xbB + h)
            p.wait(v, "WC", wcB0 + 4 * (h + 1))
            if h >= 2:
                od.wait_chunk(v, obB + h - 2)
                if want_stats:
                    p.wait(v, "PE", pe_S[h - 2])
            inst = v.scalar_tensor_tensor(x1t[h % 2][:], etmp[h % 2][:],
                                          sb["hb1t"][:, h:h + 1],
                                          XS[h % 2][:], ALU.add, ALU.add)
            p.inc("VE", inst)
            if want_stats:
                v.drain()
                v.tensor_copy(XH[:, T * h:T * (h + 1)], x1t[h % 2][:])
                v.drain()
                inst = v.tensor_tensor(sb["xsq"][h % 2][:], x1t[h % 2][:],
                                       x1t[h % 2][:], ALU.mult)
                p.inc("VE", inst, 2)

    scope_b.__exit__(None, None, None)
    return pe_S[NH - 1] if want_stats else None


def emit_proj_rope(p, sb, ws, w_chunks, nout, dst, do_rope):
    blk, PS = p.blk, sb["PS"]
    XH, fscr = sb["XH"], sb["fscr"]
    ws.load_run(w_chunks, nout)
    ac0 = p.cnt["AC"]
    wcP0 = ws.n_used
    gpP0 = p.cnt["GP"]

    @blk.tensor
    def _(t):
        p.wait(t, "AC", ac0)
        p.wait(t, "GP", gpP0)
        for i in range(nout):
            w = ws.acquire(t)
            if i >= 2:
                p.wait(t, "AC", ac0 + 2 * (i - 1))
            a, b = PS[2 * (i % 2)], PS[2 * (i % 2) + 1]
            for k in range(NK):
                t.matmul(a[:], w[:, 128 * k:128 * (k + 1)],
                         XH[:, T * k:T * k + 512],
                         start=(k == 0), stop=(k == NK - 1))
                mm = t.matmul(b[:], w[:, 128 * k:128 * (k + 1)],
                              XH[:, T * k + 512:T * (k + 1)],
                              start=(k == 0), stop=(k == NK - 1))
            ws.release(mm)

    @blk.scalar
    def _(s):
        for i in range(nout):
            p.wait(s, "WC", wcP0 + i + 1)
            a, b = PS[2 * (i % 2)], PS[2 * (i % 2) + 1]
            s.copy(dst[:, T * i:T * i + 512], a[:])
            inst = s.copy(dst[:, T * i + 512:T * (i + 1)], b[:])
            p.inc("AC", inst, 2)
    ac_ev = p.cnt["AC"]

    if not do_rope:
        return

    peR0 = p.cnt["PE"]
    veR0 = p.cnt["VE"]

    @blk.tensor
    def _(t):
        p.wait(t, "GD")
        for idx in range(2 * nout):
            i, x = divmod(idx, 2)
            p.wait(t, "AC", ac0 + 2 * (i + 1))
            if idx >= 2:
                p.wait(t, "VE", veR0 + 3 * (idx - 1))
            mm = t.matmul(PS[4 + (idx % 2)][:, :], sb["pmb"][:],
                          dst[:, T * i + 512 * x:T * i + 512 * (x + 1)],
                          start=True, stop=True)
            p.inc("PE", mm)

    @blk.vector
    def _(v):
        for idx in range(2 * nout):
            i, x = divmod(idx, 2)
            p.wait(v, "PE", peR0 + idx + 1)
            sl = slice(T * i + 512 * x, T * i + 512 * (x + 1))
            csl = slice(512 * x, 512 * (x + 1))
            v.drain()
            v.tensor_tensor(fscr[0][:, 0:512], dst[:, sl],
                            sb["cosT"][:, csl], ALU.mult)
            v.tensor_tensor(fscr[1][:, 0:512], PS[4 + (idx % 2)][:, :],
                            sb["sinT"][:, csl], ALU.mult)
            v.drain()
            inst = v.tensor_tensor(dst[:, sl], fscr[0][:, 0:512],
                                   fscr[1][:, 0:512], ALU.add)
            p.inc("VE", inst, 3)


def emit_kproj_chunked(p, sb, ws, wkvm, ktmp, xh_ready):
    blk, PS = p.blk, sb["PS"]
    XH = sb["XH"]
    ws.load_run(wkvm, 4)
    ac0 = p.cnt["AC"]
    wcK0 = ws.n_used

    @blk.tensor
    def _(t):
        p.wait(t, "AC", ac0)
        for g in range(4):
            w = ws.acquire(t)
            for j in range(4):
                kc = 4 * g + j
                for semn, val in xh_ready(kc):
                    p.wait(t, semn, val)
                for o in range(4):
                    wt = w[:, 512 * j + 128 * o:512 * j + 128 * (o + 1)]
                    t.matmul(PS[2 * o][:, :], wt, XH[:, T * kc:T * kc + 512],
                             start=(kc == 0), stop=(kc == NK - 1))
                    mm = t.matmul(PS[2 * o + 1][:, :], wt,
                                  XH[:, T * kc + 512:T * (kc + 1)],
                                  start=(kc == 0), stop=(kc == NK - 1))
            p.inc("WC", mm, 1)
            ws.n_used += 1

    @blk.scalar
    def _(s):
        p.wait(s, "WC", wcK0 + 4)
        for idx in range(8):
            o, x = divmod(idx, 2)
            inst = s.copy(ktmp[idx], PS[2 * o + x][:, :])
        p.inc("AC", inst)


def emit_krope(p, sb, ktmp, dst):
    blk, PS, fscr = p.blk, sb["PS"], sb["fscr"]
    ac_ev = p.cnt["AC"]
    peR0 = p.cnt["PE"]
    veR0 = p.cnt["VE"]
    xdiR = p.cnt["XDI"]

    @blk.tensor
    def _(t):
        p.wait(t, "GD")
        p.wait(t, "AC", ac_ev)
        for idx in range(8):
            if idx >= 2:
                p.wait(t, "VE", veR0 + 3 * (idx - 1))
            mm = t.matmul(PS[4 + (idx % 2)][:, :], sb["pmb"][:],
                          ktmp[idx], start=True, stop=True)
            p.inc("PE", mm)

    @blk.vector
    def _(v):
        p.wait(v, "XDI", xdiR)
        for idx in range(8):
            i, x = divmod(idx, 2)
            p.wait(v, "PE", peR0 + idx + 1)
            sl = slice(T * i + 512 * x, T * i + 512 * (x + 1))
            csl = slice(512 * x, 512 * (x + 1))
            v.drain()
            v.tensor_tensor(fscr[0][:, 0:512], ktmp[idx],
                            sb["cosT"][:, csl], ALU.mult)
            v.tensor_tensor(fscr[1][:, 0:512], PS[4 + (idx % 2)][:, :],
                            sb["sinT"][:, csl], ALU.mult)
            v.drain()
            inst = v.tensor_tensor(dst[:, sl], fscr[0][:, 0:512],
                                   fscr[1][:, 0:512], ALU.add)
            p.inc("VE", inst, 3)


def make_sbufs(nc):
    A = lambda n, s, dt: nc.alloc_sbuf_tensor(n, s, dt).ap()
    sb = {}
    sb["Dreg"] = A("Dreg", [128, NJ * T], BF16)
    sb["XH"] = A("XH", [128, NK * T], BF16)
    sb["XS"] = [A(f"XS{i}", [128, T], F32) for i in range(2)]
    sb["wbuf"] = [A(f"wbuf{i}", [128, 4096], BF16) for i in range(2)]
    sb["xsq"] = [A(f"xsq{i}", [128, T], BF16) for i in range(2)]
    sb["fscr"] = [A(f"fscr{i}", [128, T], F32) for i in range(2)]
    sb["etmp"] = sb["fscr"]
    sb["x1t"] = [A(f"x1t{i}", [128, T], F32) for i in range(2)]
    sb["Rt"] = sb["x1t"][0]
    sb["Bt"] = sb["x1t"][1]
    sb["onesb"] = A("onesb", [128, 1], BF16)
    sb["onesPf"] = A("onesPf", [1, 128], F32)
    sb["b0t"] = A("b0t", [128, NJ], F32)
    sb["b1t"] = A("b1t", [128, NH], F32)
    sb["hb1t"] = A("hb1t", [128, NH], F32)
    sb["pmb"] = A("pmb", [128, 128], BF16)
    sb["epst"] = A("epst", [1, 1], F32)
    sb["PS"] = [nc.alloc_psum_tensor(f"P{i}", [128, 512], F32).ap()
                for i in range(8)]
    vf = sb["Dreg"].bitcast(F32)
    sb["vf"] = vf
    sb["rows"] = [vf[0:1, 25088 + 1024 * i:25088 + 1024 * (i + 1)]
                  for i in range(3)]
    sb["cosT"] = vf[:, 28160:29184]
    sb["sinT"] = vf[:, 29184:30208]
    return sb


def emit_init(p, sb, loads_f32, loads_cast):
    blk = p.blk

    @blk.sync
    def _(sync):
        for dst, src in loads_f32:
            p.seq_dma("XDI", sync, dst[:], src)
    init_xd = p.cnt["XDI"]

    @blk.gpsimd
    def _(g):
        for dst, src in loads_cast:
            p.seq_dma("GD", g, dst[:], src)

    @blk.vector
    def _(v):
        v.memset(sb["onesb"][:], 1.0)
        v.memset(sb["onesPf"][:], 1.0)
        v.memset(sb["epst"][:], EPS)
        p.wait(v, "XDI", init_xd)
        inst = v.tensor_scalar(sb["hb1t"][:], sb["b1t"][:], 0.5, None, ALU.mult)
        p.inc("VE", inst)


def build_launch1():
    nc = bass.Bass("TRN2", target_bir_lowering=False, debug=False)
    f = lambda n, s, dt=F32: nc.dram_tensor(n, s, dt, kind="ExternalInput").ap()
    o = lambda n, s, dt=F32: nc.dram_tensor(n, s, dt, kind="ExternalOutput").ap()

    x0 = f("x0", [NK, 128, T])
    w0m = f("w0m", [NJ, 128, 2048])
    w1m = f("w1m", [NH, 4, 128, 2048])
    b0c = f("b0c", [128, NJ])
    b1c = f("b1c", [128, NH])
    wkvm = f("wkvm", [4, 128, 2048])
    wvm = f("wvm", [NKV, 128, 2048])
    wiqm = f("wiqm", [128, 2048])
    wikm = f("wikm", [128, 2048])
    cwab = f("cwab", [128, 2])
    cost = f("cost", [128, T])
    sint = f("sint", [128, T])
    pmat = f("pmat", [128, 128])

    x1_out = o("x1_out", [NH, 128, T])
    xh1_out = o("xh1_out", [NK, 128, T], BF16)
    ck_out = o("ck_out", [128, NKV * 64], BF16)
    cv_out = o("cv_out", [128, NKV * 64], BF16)
    ki_out = o("ki_out", [128, 1])
    qi_out = o("qi_out", [128, 512])

    sb = make_sbufs(nc)
    A = lambda n, s, dt: nc.alloc_sbuf_tensor(n, s, dt).ap()
    cwb = A("cwb", [128, 2], BF16)
    xsum = A("xsum", [128, NK], F32)
    kiT = A("kiT", [128, 1], F32)
    vf = sb["vf"]
    Dg = sb["Dreg"]
    xhf = vf[:, 0:NK * 512]
    krot = Dg[:, 0:NKV * T]
    vreg = Dg[:, NKV * T:2 * NKV * T]
    wiqt = vf[:, 8192:10240]
    wikt = vf[:, 10240:12288]
    eaT = vf[0:1, 12288:16384]
    ebT = vf[0:1, 16384:20480]
    wrow = vf[0:1, 20480:24576]
    cks = vf[:, 24576:24832]
    cvs = vf[:, 24832:25088]
    qiT = vf[:, 30208:30720]
    ztr = [vf[0:1, 25088:29184]]
    za = vf[0:1, 30720:30976]
    zb = vf[0:1, 30976:31232]
    zsc = vf[0:1, 31232:31488]
    wt1 = vf[0:1, 31488:31744]
    wt2 = vf[0:1, 31744:32000]
    ckb = Dg[:, 64000:64256]
    cvb = Dg[:, 64256:64512]

    with contextlib.ExitStack() as stack:
        blk = stack.enter_context(nc.Block())
        p = Prog(nc, stack, blk)
        ws = WStream(p, sb["wbuf"])
        PS = sb["PS"]

        emit_init(p, sb,
                  [(sb["b0t"], b0c), (sb["b1t"], b1c)],
                  [(cwb, cwab), (sb["pmb"], pmat)])

        xs = DStream(p, "XD0", "XD1")
        od = DStream(p, "OD0", "OD1")
        oh = DStream(p, "OH0", "OH1")

        with nc.named_scope("L1_ln0"):
            xh_rdy = emit_ln_simple(p, sb, xs, x0)
        pe_stats = emit_ffn(p, sb, ws, xs, od, w0m,
                            [w1m[h] for h in range(NH)],
                            x0, x1_out, want_stats=True, tag="L1_ffn1",
                            xh_ready=xh_rdy)
        with nc.named_scope("L1_rows1"):
            ac_rows = emit_ln_rows(p, sb, pe_stats)

        ve1 = p.cnt["VE"]
        xb1 = xs.n
        xdi1 = p.cnt["XDI"]

        wd2_x1 = p.cnt["WD2"]

        @blk.sync
        def _(sync):
            p.wait(sync, "PE", pe_stats)
            p.seq_dma("XDI", sync, wiqt, wiqm)
            p.seq_dma("XDI", sync, wikt, wikm)
            p.seq_dma("XDI", sync, sb["cosT"], cost)
            p.seq_dma("XDI", sync, sb["sinT"], sint)
            for k in range(0, NK, 2):
                if k >= 2:
                    p.wait(sync, "VE", ve1 + 2 * (k - 1))
                else:
                    p.wait(sync, "VE", ve1)
                xs.dma(sync, sb["XS"][0][:], x1_out[k])

        ac1b = p.cnt["AC"]
        fscr = sb["fscr"]

        @blk.vector
        def _(v):
            p.wait(v, "AC", ac_rows)
            od.wait_all(v)
            fscr = sb["fscr"]
            for k in range(NK):
                if k % 2 == 0:
                    xs.wait_chunk(v, xb1 + k // 2)
                else:
                    p.wait(v, "WD2", wd2_x1 + 16 * (k // 2 + 1))
                if k >= 2:
                    p.wait(v, "AC", ac1b + (k - 1))
                v.tensor_tensor(fscr[k % 2][:], sb["XS"][k % 2][:],
                                sb["Rt"][:, :], ALU.mult)
                v.drain()
                v.tensor_tensor(fscr[k % 2][:], fscr[k % 2][:],
                                sb["Bt"][:, :], ALU.add)
                v.drain()
                inst = v.tensor_copy(xhf[:, 512 * k:512 * (k + 1)],
                                     fscr[k % 2][:, 0:512])
                p.inc("VE", inst, 2)

        @blk.scalar
        def _(s):
            for k in range(NK):
                p.wait(s, "VE", ve1 + 2 * (k + 1))
                if k % 2 == 0 and k + 1 < NK:
                    p.seq_dma("WD2", s, sb["XS"][1][:], x1_out[k + 1])
                inst = s.activation(sb["XH"][:, T * k:T * (k + 1)],
                                    fscr[k % 2][:], AF.Copy, bias=0.0,
                                    scale=1.0, accum_out=xsum[:, k:k + 1])
                p.inc("AC", inst)
                s.drain()
                oh.dma(s, xh1_out[k], sb["XH"][:, T * k:T * (k + 1)])

        XSb = [sb["XS"][i].bitcast(BF16) for i in range(2)]
        ktmp = [XSb[idx // 4][:, 512 * (idx % 4):512 * (idx % 4 + 1)]
                for idx in range(8)]
        with nc.named_scope("L1_kv"):
            emit_kproj_chunked(p, sb, ws, wkvm, ktmp,
                               lambda kc: [("AC", ac1b + kc + 1)])

        @blk.tensor
        def _(t):
            p.wait(t, "VE")
            p.wait(t, "AC")
            p.wait(t, "XDI", xdi1 + 64)
            for k in range(NK):
                t.matmul(PS[4][:, 0:1], wikt[:, 128 * k:128 * (k + 1)],
                         xsum[:, k:k + 1], start=(k == 0), stop=(k == NK - 1))
            for k in range(NK):
                mm = t.matmul(PS[5][:, :], wiqt[:, 128 * k:128 * (k + 1)],
                              xhf[:, 512 * k:512 * (k + 1)],
                              start=(k == 0), stop=(k == NK - 1))
            p.inc("PE", mm)
        pe_kiqi = p.cnt["PE"]

        @blk.scalar
        def _(s):
            p.wait(s, "PE", pe_kiqi)
            s.copy(kiT[:], PS[4][:, 0:1])
            inst = s.copy(qiT[:], PS[5][:, :])
            p.inc("AC", inst)
            s.drain()
            p.seq_dma("ODM", s, ki_out, kiT[:])
            p.seq_dma("ODM", s, qi_out, qiT[:])

        with nc.named_scope("L1_krope"):
            emit_krope(p, sb, ktmp, krot)

        ve_all = p.cnt["VE"]
        acC0 = p.cnt["AC"]
        peC0 = p.cnt["PE"]

        @blk.tensor
        def _(t):
            p.wait(t, "VE", ve_all)
            p.wait(t, "AC", acC0)
            p.wait(t, "GD")
            for idx in range(2 * NKV):
                i, x = divmod(idx, 2)
                if idx >= 1:
                    p.wait(t, "AC", acC0 + 2 * idx)
                sl = slice(i * T + 512 * x, i * T + 512 * (x + 1))
                t.matmul(PS[4][0:1, :], cwb[:, 0:1], krot[:, sl],
                         start=True, stop=True)
                mm = t.matmul(PS[5][0:1, :], cwb[:, 1:2], krot[:, sl],
                              start=True, stop=True)
                p.inc("PE", mm)

        @blk.scalar
        def _(s):
            for idx in range(2 * NKV):
                i, x = divmod(idx, 2)
                p.wait(s, "PE", peC0 + idx + 1)
                sl = slice(i * T + 512 * x, i * T + 512 * (x + 1))
                s.activation(eaT[:, sl], PS[4][0:1, :], AF.Exp)
                inst = s.activation(ebT[:, sl], PS[5][0:1, :], AF.Exp)
                p.inc("AC", inst, 2)
        ac_exp = p.cnt["AC"]

        @blk.vector
        def _(v):
            p.wait(v, "AC", ac_exp)
            for (src, dst) in ((eaT, za), (ebT, zb)):
                v.tensor_tensor(ztr[0][:, 0:2048], src[:, 0::2], src[:, 1::2],
                                ALU.add)
                v.drain()
                v.tensor_tensor(ztr[0][:, 2048:3072], ztr[0][:, 0:2048:2],
                                ztr[0][:, 1:2048:2], ALU.add)
                v.drain()
                v.tensor_tensor(ztr[0][:, 3072:3584], ztr[0][:, 2048:3072:2],
                                ztr[0][:, 2049:3072:2], ALU.add)
                v.drain()
                v.tensor_tensor(zsc[:], ztr[0][:, 3072:3584:2],
                                ztr[0][:, 3073:3584:2], ALU.add)
                v.drain()
                v.reciprocal(dst[:], zsc[:])
                v.drain()
            for r in range(16):
                v.tensor_tensor(wrow[:, r::16], eaT[:, r::16], za[:], ALU.mult)
            v.drain()
            for r in range(16):
                v.tensor_tensor(eaT[:, r::16], ebT[:, r::16], zb[:], ALU.mult)
            v.drain()
            v.tensor_tensor(wrow[:], wrow[:], eaT[:], ALU.add)
            v.drain()
            inst = v.tensor_scalar(wrow[:], wrow[:], 0.5, None, ALU.mult)
            v.drain()
            p.inc("VE", inst)
        ve_wrow = p.cnt["VE"]


        emit_proj_rope(p, sb, ws, wvm, NKV, vreg, False)

        peW0 = p.cnt["PE"]
        veW0 = p.cnt["VE"]

        @blk.tensor
        def _(t):
            p.wait(t, "VE", ve_wrow)
            for idx in range(2 * NKV):
                i, x = divmod(idx, 2)
                if idx >= 2:
                    p.wait(t, "VE", veW0 + (idx - 1))
                sl = slice(i * T + 512 * x, i * T + 512 * (x + 1))
                mm = t.matmul(PS[6 + (idx % 2)][:, :], sb["onesPf"][:],
                              wrow[:, sl], start=True, stop=True)
                p.inc("PE", mm)

        def tree_sum(eng, f, seg):
            eng.tensor_tensor(f[:, 512:768], f[:, 0:512:2], f[:, 1:512:2],
                              ALU.add)
            eng.drain()
            eng.tensor_tensor(f[:, 768:896], f[:, 512:768:2], f[:, 513:768:2],
                              ALU.add)
            eng.drain()
            eng.tensor_tensor(f[:, 896:960], f[:, 768:896:2], f[:, 769:896:2],
                              ALU.add)
            eng.drain()
            eng.tensor_tensor(seg[:, :], f[:, 896:960:2], f[:, 897:960:2],
                              ALU.add)

        ve_kw = [None] * (2 * NKV)
        gp0 = p.cnt["GP"]

        @blk.vector
        def _(v):
            fscr = sb["fscr"]
            p.wait(v, "AC")
            for idx in range(2 * NKV):
                i, x = divmod(idx, 2)
                sl = slice(i * T + 512 * x, i * T + 512 * (x + 1))
                p.wait(v, "PE", peW0 + idx + 1)
                if idx >= 1:
                    p.wait(v, "GP", gp0 + idx)
                v.drain()
                v.tensor_tensor(fscr[0][:, 0:512], krot[:, sl],
                                PS[6 + (idx % 2)][:, :], ALU.mult)
                inst = v.tensor_tensor(fscr[1][:, 0:512], vreg[:, sl],
                                       PS[6 + (idx % 2)][:, :], ALU.mult)
                ve_kw[idx] = p.inc("VE", inst)
                v.drain()
                tree_sum(v, fscr[0],
                         cks[:, 64 * i + 32 * x:64 * i + 32 * (x + 1)])
                v.drain()
            inst = v.tensor_copy(ckb[:], cks[:])
            p.inc("VE", inst)
        ve_done = p.cnt["VE"]

        @blk.gpsimd
        def _(g):
            for idx in range(2 * NKV):
                p.wait(g, "VE", ve_kw[idx])
                i, x = divmod(idx, 2)
                tree_sum(g, sb["fscr"][1],
                         cvs[:, 64 * i + 32 * x:64 * i + 32 * (x + 1)])
                g.drain()
                inst = g.engine_nop()
                p.inc("GP", inst, 1)
            g.drain()
            inst = g.tensor_copy(cvb[:], cvs[:])
            g.drain()
            inst = g.engine_nop()
            p.inc("GP", inst, 1)
        gp_done = p.cnt["GP"]

        @blk.scalar
        def _(s):
            p.wait(s, "VE", ve_done)
            p.wait(s, "GP", gp_done)
            p.seq_dma("ODM", s, ck_out, ckb[:])
            p.seq_dma("ODM", s, cv_out, cvb[:])

        @blk.sync
        def _(sync):
            od.wait_all(sync)
            oh.wait_all(sync)
            p.wait(sync, "ODM")

    return nc


def build_launch2():
    nc = bass.Bass("TRN2", target_bir_lowering=False, debug=False)
    f = lambda n, s, dt=F32: nc.dram_tensor(n, s, dt, kind="ExternalInput").ap()

    x1 = f("x1", [NH, 128, T])
    xh1 = f("xh1", [NK, 128, T], BF16)
    cktg = f("cktg", [NKV, 4, 128, 128], BF16)
    cvt = f("cvt", [NKV, 4, 128, 128], BF16)
    ltc = f("ltc", [4, 128, 512], BF16)
    iota64 = f("iota64", [128, 64])
    kis = f("kis", [128, NCORE])
    qi0 = f("qi0", [128, 512])
    wqm = f("wqm", [NH, 128, 2048])
    wom = f("wom", [NH, 128, 2048])
    w0m = f("w0m", [NJ, 128, 2048])
    w1m = f("w1m", [NH, 4, 128, 2048])
    b0c = f("b0c", [128, NJ])
    b1c = f("b1c", [128, NH])
    cost = f("cost", [128, T])
    sint = f("sint", [128, T])
    pmat = f("pmat", [128, 128])

    y_out = nc.dram_tensor("y_out", [NH, 128, T], F32,
                           kind="ExternalOutput").ap()
    x2scr = nc.dram_tensor("x2scr", [NH, 128, T], F32, kind="Internal").ap()

    sb = make_sbufs(nc)
    vf = sb["vf"]
    Dg = sb["Dreg"]
    qrot = Dg[:, 0:NH * T]
    attn = Dg[:, NH * T:2 * NH * T]
    cktt = Dg[:, 32768:34816]
    cvtt = Dg[:, 34816:36864]
    ebuf = [Dg[0:64, 36864 + 512 * i:36864 + 512 * (i + 1)] for i in range(2)]
    ck64 = Dg[:, 37888:38144]
    cv64 = Dg[0:64, 38144:38656]
    Pt = Dg[:, 38656:38912]
    LTt = Dg[:, 38912:40960]
    Bsb = vf[:, 21504:22016]
    cgt = vf[:, 22016:22528]
    Rsb = [vf[:, 22528:23040], vf[:, 23040:23552]]
    srow = vf[0:1, 23552:24064]
    denr = [vf[0:1, 24064:24576], vf[0:1, 24576:25088]]
    qi0t = vf[:, 25088:25600]
    kist = vf[:, 25600:25600 + NCORE]
    kisum = vf[:, 25632:25633]
    scol = vf[:, 25664:25668]
    cnt_ = vf[:, 25696:25700]
    maskc = vf[:, 25728:25732]
    mkb = Dg[:, 51520:51524]
    posc = vf[:, 25792:25796]
    iotat = vf[:, 25824:25888]

    with contextlib.ExitStack() as stack:
        blk = stack.enter_context(nc.Block())
        p = Prog(nc, stack, blk)
        ws = WStream(p, sb["wbuf"])
        PS = sb["PS"]

        xs = DStream(p, "XD0", "XD1")
        od = DStream(p, "OD0", "OD1")

        @blk.sync
        def _(sync):
            for k in range(NK):
                p.seq_dma("XDI", sync, sb["XH"][:, T * k:T * (k + 1)], xh1[k])
        xd_data = p.cnt["XDI"]

        emit_init(p, sb,
                  [(sb["cosT"], cost), (sb["sinT"], sint), (qi0t, qi0),
                   (kist, kis), (sb["b0t"], b0c), (sb["b1t"], b1c)],
                  [(sb["pmb"], pmat)])
        xd_all = p.cnt["XDI"]

        @blk.sync
        def _(sync):
            p.seq_dma("XDI", sync, iotat, iota64)
            for i in range(NKV):
                p.seq_dma(
                    "XDI", sync,
                    cktt[:, 512 * i:512 * (i + 1)].rearrange(
                        "p (c t) -> p c t", c=4),
                    cktg[i].rearrange("c p t -> p c t"))
                p.seq_dma(
                    "XDI", sync,
                    cvtt[:, 512 * i:512 * (i + 1)].rearrange(
                        "p (c t) -> p c t", c=4),
                    cvt[i].rearrange("c p t -> p c t"))
        xd_iota = xd_all + 16
        xd_ckcv = p.cnt["XDI"]

        @blk.sync
        def _(sync):
            for jc in range(4):
                p.seq_dma("XDI", sync, LTt[:, 512 * jc:512 * (jc + 1)],
                          ltc[jc])
        xd_lt = p.cnt["XDI"]

        @blk.tensor
        def _(t):
            p.wait(t, "XDI", xd_data)

        @blk.vector
        def _(v):
            p.wait(v, "XDI", xd_all)

        with nc.named_scope("L2_qproj"):
            emit_proj_rope(p, sb, ws, wqm, NH, qrot, True)
        ve_q = p.cnt["VE"]

        @blk.vector
        def _(v):
            inst = v.reduce_sum(kisum[:], kist[:], axis=AX)
            p.inc("VE", inst)
        ve_ki = p.cnt["VE"]

        @blk.tensor
        def _(t):
            p.wait(t, "VE", ve_ki)
            p.wait(t, "VE", ve_q)
            t.matmul(PS[4][0:1, :], kisum[:], qi0t[:], start=True, stop=True)
            for c in range(4):
                mm = t.matmul(PS[5][:, c:c + 1],
                              qi0t[:, 128 * c:128 * (c + 1)], kisum[:],
                              start=True, stop=True)
            p.inc("PE", mm)
        pe_s1 = p.cnt["PE"]

        @blk.scalar
        def _(s):
            p.wait(s, "PE", pe_s1)
            s.copy(srow[:], PS[4][0:1, :])
            inst = s.copy(scol[:], PS[5][:, 0:4])
            p.inc("AC", inst)
        ac_s1 = p.cnt["AC"]

        @blk.tensor
        def _(t):
            p.wait(t, "AC", ac_s1)
            mm = t.matmul(PS[6][:, :], sb["onesPf"][:], srow[:],
                          start=True, stop=True)
            p.inc("PE", mm)
        pe_s2 = p.cnt["PE"]

        @blk.scalar
        def _(s):
            p.wait(s, "PE", pe_s2)
            inst = s.copy(Bsb[:], PS[6][:, :])
            p.inc("AC", inst)
        ac_s2 = p.cnt["AC"]

        @blk.vector
        def _(v):
            p.wait(v, "AC", ac_s2)
            for c in range(4):
                v.tensor_scalar(cgt[:], Bsb[:], scol[:, c:c + 1], None,
                                ALU.is_gt)
                v.drain()
                v.reduce_sum(cnt_[:, c:c + 1], cgt[:], axis=AX)
                v.drain()
            v.tensor_single_scalar(maskc[:], cnt_[:], 63.5, ALU.is_lt)
            v.drain()
            inst = v.tensor_copy(mkb[:], maskc[:])
            v.drain()
            p.inc("VE", inst)
        ve_mask = p.cnt["VE"]

        @blk.tensor
        def _(t):
            p.wait(t, "VE", ve_mask)
            p.wait(t, "XDI", xd_lt)
            for nc_ in range(4):
                for jc in range(nc_ + 1):
                    mm = t.matmul(PS[7][:, nc_:nc_ + 1],
                                  LTt[:, 512 * jc + 128 * nc_:
                                      512 * jc + 128 * (nc_ + 1)],
                                  mkb[:, jc:jc + 1],
                                  start=(jc == 0), stop=(jc == nc_))
            p.inc("PE", mm)
        pe_pos = p.cnt["PE"]

        @blk.scalar
        def _(s):
            p.wait(s, "PE", pe_pos)
            inst = s.copy(posc[:], PS[7][:, 0:4])
            p.inc("AC", inst)
        ac_pos = p.cnt["AC"]

        @blk.vector
        def _(v):
            p.wait(v, "AC", ac_pos)
            p.wait(v, "XDI", xd_iota)
            for c in range(4):
                v.tensor_scalar(cgt[:, 0:64], iotat[:], posc[:, c:c + 1],
                                None, ALU.is_equal)
                v.drain()
                v.tensor_scalar(Pt[:, 64 * c:64 * (c + 1)], cgt[:, 0:64],
                                maskc[:, c:c + 1], None, ALU.mult)
                v.drain()
            inst = v.engine_nop()
            p.inc("VE", inst)
        ve_P = p.cnt["VE"]

        ac_g0 = p.cnt["AC"]

        @blk.tensor
        def _(t):
            p.wait(t, "VE", ve_P)
            p.wait(t, "XDI", xd_ckcv)
            for i in range(NKV):
                if i >= 1:
                    p.wait(t, "AC", ac_g0 + i)
                for c in range(4):
                    sl = slice(512 * i + 128 * c, 512 * i + 128 * (c + 1))
                    t.matmul(PS[2][:, 0:64], cktt[:, sl],
                             Pt[:, 64 * c:64 * (c + 1)],
                             start=(c == 0), stop=(c == 3))
                    mm = t.matmul(PS[3][0:64, 0:128],
                                  Pt[:, 64 * c:64 * (c + 1)],
                                  cvtt[:, sl], start=(c == 0), stop=(c == 3))
                p.inc("PE", mm)
        pe_g0 = p.cnt["PE"] - NKV

        @blk.scalar
        def _(s):
            for i in range(NKV):
                p.wait(s, "PE", pe_g0 + i + 1)
                s.copy(ck64[:, 64 * i:64 * (i + 1)], PS[2][:, 0:64])
                inst = s.copy(cv64[:, 128 * i:128 * (i + 1)],
                              PS[3][0:64, 0:128])
                p.inc("AC", inst)
        @blk.tensor
        def _(t):
            pass
        ac_g = p.cnt["AC"]

        NIT = 2 * NH
        peA0 = p.cnt["PE"]
        acA0 = p.cnt["AC"]
        veA0 = p.cnt["VE"]

        def sched(pairs):
            out = []
            for s_ in range(NIT + 3):
                for kind, lag in pairs:
                    it = s_ - lag
                    if 0 <= it < NIT:
                        out.append((kind, it))
            return out

        pe_sched = sched([("lg", 0), ("denav", 1), ("R", 2)])
        ac_sched = sched([("exp", 0), ("Rc", 2)])
        ve_sched = sched([("out", 3), ("rc", 1)])
        pe_at, ac_at, ve_at = {}, {}, {}
        c = peA0
        for ev in pe_sched:
            c += 1
            pe_at[ev] = c
        c = acA0
        for ev in ac_sched:
            c += 1
            ac_at[ev] = c
        c = veA0
        for ev in ve_sched:
            c += 1
            ve_at[ev] = c

        def itsl(it):
            h, x = divmod(it, 2)
            return h // 4, slice(T * h + 512 * x, T * h + 512 * (x + 1))

        @blk.tensor
        def _(t):
            p.wait(t, "AC", ac_g)
            for kind, it in pe_sched:
                pr = it % 2
                hk, qsl = itsl(it)
                if kind == "lg":
                    if it >= 2:
                        p.wait(t, "AC", ac_at[("exp", it - 2)])
                    mm = t.matmul(PS[pr][0:64, :],
                                  ck64[:, 64 * hk:64 * (hk + 1)],
                                  qrot[:, qsl], start=True, stop=True)
                    p.inc("PE", mm)
                elif kind == "denav":
                    p.wait(t, "AC", ac_at[("exp", it)])
                    if it >= 2:
                        p.wait(t, "VE", ve_at[("out", it - 2)])
                    t.matmul(PS[4 + pr][0:1, :], sb["onesb"][0:64, :],
                             ebuf[pr][:], start=True, stop=True)
                    mm = t.matmul(PS[2 + pr][:, :],
                                  cv64[:, 128 * hk:128 * (hk + 1)],
                                  ebuf[pr][:], start=True, stop=True)
                    p.inc("PE", mm)
                else:
                    p.wait(t, "VE", ve_at[("rc", it)])
                    mm = t.matmul(PS[6 + pr][:, :], sb["onesPf"][:],
                                  denr[pr][:], start=True, stop=True)
                    p.inc("PE", mm)

        @blk.scalar
        def _(s):
            for kind, it in ac_sched:
                pr = it % 2
                if kind == "exp":
                    p.wait(s, "PE", pe_at[("lg", it)])
                    inst = s.activation(ebuf[pr][:], PS[pr][0:64, :], AF.Exp,
                                        bias=0.0, scale=ISQ)
                    p.inc("AC", inst)
                else:
                    p.wait(s, "PE", pe_at[("R", it)])
                    if it >= 2:
                        p.wait(s, "VE", ve_at[("out", it - 2)])
                    inst = s.copy(Rsb[pr][:], PS[6 + pr][:, :])
                    p.inc("AC", inst)

        @blk.vector
        def _(v):
            for kind, it in ve_sched:
                pr = it % 2
                hk, qsl = itsl(it)
                if kind == "rc":
                    p.wait(v, "PE", pe_at[("denav", it)])
                    inst = v.reciprocal(denr[pr][:], PS[4 + pr][0:1, :])
                    p.inc("VE", inst)
                else:
                    p.wait(v, "AC", ac_at[("Rc", it)])
                    inst = v.tensor_tensor(attn[:, qsl], PS[2 + pr][:, :],
                                           Rsb[pr][:], ALU.mult)
                    p.inc("VE", inst)

        scope_wo = nc.named_scope("L2_wo")
        scope_wo.__enter__()
        ws.load_run(wom, NH)
        ve_attn = p.cnt["VE"]
        wcW0 = ws.n_used
        pe_S = [None] * NH
        xbW = xs.n
        obW = od.n

        @blk.sync
        def _(sync):
            for h in range(NH):
                if h >= 2:
                    p.wait(sync, "VE", ve_attn + 3 * (h - 1))
                xs.dma(sync, sb["XS"][h % 2][:], x1[h])

        @blk.tensor
        def _(t):
            for h in range(NH):
                a, b = PS[2 * (h % 2)], PS[2 * (h % 2) + 1]
                if h == 0:
                    p.wait(t, "AC", ac_at[("exp", NIT - 2)])
                    p.wait(t, "AC", ac_at[("exp", NIT - 1)])
                elif h == 1:
                    p.wait(t, "VE", ve_at[("out", NIT - 1)])
                else:
                    p.wait(t, "VE", ve_attn + 3 * (h - 1))
                w = ws.acquire(t)
                for k in range(NK):
                    if h == 0:
                        p.wait(t, "VE", ve_at[("out", 2 * k + 1)])
                    t.matmul(a[:], w[:, 128 * k:128 * (k + 1)],
                             attn[:, T * k:T * k + 512],
                             start=(k == 0), stop=(k == NK - 1))
                    mm = t.matmul(b[:], w[:, 128 * k:128 * (k + 1)],
                                  attn[:, T * k + 512:T * (k + 1)],
                                  start=(k == 0), stop=(k == NK - 1))
                ws.release(mm)
                if h >= 1:
                    p.wait(t, "VE", ve_attn + 3 * h)
                    mm = emit_stats_mms(p, t, PS, sb["onesb"],
                                        sb["XH"][:, T * (h - 1):T * h],
                                        sb["xsq"][(h - 1) % 2],
                                        h - 1 == 0, False)
                    pe_S[h - 1] = p.inc("PE", mm)
            p.wait(t, "VE", ve_attn + 3 * NH)
            mm = emit_stats_mms(p, t, PS, sb["onesb"],
                                sb["XH"][:, T * (NH - 1):T * NH],
                                sb["xsq"][(NH - 1) % 2], False, True)
            pe_S[NH - 1] = p.inc("PE", mm)

        @blk.vector
        def _(v):
            for h in range(NH):
                p.wait(v, "WC", wcW0 + h + 1)
                xs.wait_chunk(v, xbW + h)
                if h >= 2:
                    od.wait_chunk(v, obW + h - 2)
                    p.wait(v, "PE", pe_S[h - 2])
                a, b = PS[2 * (h % 2)], PS[2 * (h % 2) + 1]
                v.tensor_tensor(sb["x1t"][h % 2][:, 0:512], a[:],
                                sb["XS"][h % 2][:, 0:512], ALU.add)
                v.tensor_tensor(sb["x1t"][h % 2][:, 512:1024], b[:],
                                sb["XS"][h % 2][:, 512:1024], ALU.add)
                v.drain()
                v.tensor_copy(sb["XH"][:, T * h:T * (h + 1)],
                              sb["x1t"][h % 2][:])
                v.drain()
                inst = v.tensor_tensor(sb["xsq"][h % 2][:],
                                       sb["x1t"][h % 2][:],
                                       sb["x1t"][h % 2][:], ALU.mult)
                p.inc("VE", inst, 3)

        @blk.scalar
        def _(s):
            for h in range(NH):
                p.wait(s, "VE", ve_attn + 3 * (h + 1))
                od.dma(s, x2scr[h], sb["x1t"][h % 2][:])

        scope_wo.__exit__(None, None, None)
        with nc.named_scope("L2_rows2"):
            ac_rows2 = emit_ln_rows(p, sb, pe_S[NH - 1])

        ve2 = p.cnt["VE"]
        gp2b = p.cnt["GP"]
        xb2 = xs.n

        @blk.sync
        def _(sync):
            for k in range(NK):
                if k >= 2:
                    p.wait(sync, "VE", ve2 + 2 * (k - 1))
                    p.wait(sync, "GP", gp2b + (k - 1))
                else:
                    p.wait(sync, "VE", ve2)
                xs.dma(sync, sb["XS"][k % 2][:], x2scr[k])

        SPL = 384

        @blk.vector
        def _(v):
            p.wait(v, "AC", ac_rows2)
            od.wait_all(v)
            fscr = sb["fscr"]
            for k in range(NK):
                xs.wait_chunk(v, xb2 + k)
                if k >= 2:
                    p.wait(v, "GP", gp2b + (k - 1))
                inst = v.tensor_tensor(fscr[k % 2][:], sb["XS"][k % 2][:],
                                       sb["Rt"][:, :], ALU.mult)
                p.inc("VE", inst)
                v.drain()
                inst = v.tensor_tensor(sb["XH"][:, T * k:T * k + SPL],
                                       fscr[k % 2][:, 0:SPL],
                                       sb["Bt"][:, 0:SPL], ALU.add)
                p.inc("VE", inst)

        @blk.gpsimd
        def _(g):
            fscr = sb["fscr"]
            for k in range(NK):
                p.wait(g, "VE", ve2 + 2 * k + 1)
                inst = g.tensor_tensor(sb["XH"][:, T * k + SPL:T * (k + 1)],
                                       fscr[k % 2][:, SPL:1024],
                                       sb["Bt"][:, SPL:1024], ALU.add)
                g.drain()
                inst = g.engine_nop()
                p.inc("GP", inst)

        xh2_rdy = (lambda v2, g2: (lambda k: [("VE", v2 + 2 * (k + 1)),
                                              ("GP", g2 + (k + 1))]))(ve2, gp2b)
        emit_ffn(p, sb, ws, xs, od, w0m, [w1m[h] for h in range(NH)],
                 x2scr, y_out, want_stats=False, tag="L2_ffn3",
                 xh_ready=xh2_rdy)

        @blk.sync
        def _(sync):
            od.wait_all(sync)

    return nc


def _rope_tables(core):
    half = 64
    t = np.arange(core * T, (core + 1) * T, dtype=np.float64)
    freqs = (1.0 / 10000.0) ** (np.arange(half, dtype=np.float64) / half)
    ang = np.outer(t, freqs)
    cosv, sinv = np.cos(ang), np.sin(ang)
    cosT = np.empty((128, T), np.float32)
    sinT = np.empty((128, T), np.float32)
    cosT[0::2, :] = cosv.T
    cosT[1::2, :] = cosv.T
    sinT[0::2, :] = -sinv.T
    sinT[1::2, :] = sinv.T
    return cosT, sinT


def _layouts(inputs):
    g = {}
    x = np.asarray(inputs["L0_attn_x"])[0]
    g["x0"] = np.ascontiguousarray(
        x.reshape(NCORE, T, NK, 128).transpose(0, 2, 3, 1))
    W0 = np.asarray(inputs["L0_ffn_W0"])
    g["w0m"] = np.ascontiguousarray(
        W0.reshape(NJ, 128, NK, 128).transpose(0, 3, 2, 1)).reshape(
            NJ, 128, 2048)
    W1 = np.asarray(inputs["L0_ffn_W1"])
    g["w1m"] = np.ascontiguousarray(
        W1.reshape(NH, 128, 4, NK, 128).transpose(0, 2, 4, 3, 1)).reshape(
            NH, 4, 128, 2048)
    g["b0c"] = np.ascontiguousarray(
        np.asarray(inputs["L0_ffn_bias0"]).reshape(NJ, 128).T)
    g["b1c"] = np.ascontiguousarray(
        np.asarray(inputs["L0_ffn_bias1"]).reshape(NH, 128).T)

    def proj(w, nout):
        return np.ascontiguousarray(
            np.asarray(w).reshape(NK, 128, nout, 128).transpose(2, 1, 0, 3)
        ).reshape(nout, 128, 2048)

    g["wqm"] = proj(inputs["L0_attn_wq"], NH)
    g["wom"] = proj(inputs["L0_attn_wo"], NH)
    wk = np.asarray(inputs["L0_attn_wk"])
    g["wkvm"] = np.ascontiguousarray(
        wk.reshape(4, 4, 128, 4, 128).transpose(0, 2, 1, 3, 4)).reshape(
            4, 128, 2048)
    g["wvm"] = proj(inputs["L0_attn_wv"], NKV)
    g["wiqm"] = proj(inputs["L0_attn_wiq"], 1).reshape(128, 2048)
    g["wikm"] = proj(inputs["L0_attn_wik"], 1).reshape(128, 2048)
    g["cwab"] = np.ascontiguousarray(
        np.stack([np.asarray(inputs["L0_attn_cwa"]),
                  np.asarray(inputs["L0_attn_cwb"])], axis=1)).astype(
                      np.float32)
    pm = np.zeros((128, 128), np.float32)
    for i in range(128):
        pm[i, i ^ 1] = 1.0
    g["pmat"] = pm
    return g


def kernel(**inputs):
    g = _layouts(inputs)
    tabs = [_rope_tables(c) for c in range(NCORE)]

    nc1 = build_launch1()
    shared1 = {k: g[k] for k in ["w0m", "w1m", "b0c", "b1c", "wkvm", "wvm",
                                 "wiqm", "wikm", "cwab", "pmat"]}
    in1 = [dict(shared1, x0=g["x0"][c], cost=tabs[c][0], sint=tabs[c][1])
           for c in range(NCORE)]
    r1 = run_bass_kernel_spmd(nc1, in1, core_ids=list(range(NCORE))).results

    ck = np.stack([r["ck_out"] for r in r1])
    cv = np.stack([r["cv_out"] for r in r1])

    def nmajor(a):
        t = a.reshape(NCORE, 128, NKV, 64).transpose(2, 0, 3, 1)
        return np.ascontiguousarray(t).reshape(NKV, 4, 128, 128)

    cktg = nmajor(ck)
    cvt = nmajor(cv)
    kis = np.concatenate([r["ki_out"] for r in r1], axis=1)
    qi0 = r1[0]["qi_out"]

    nc2 = build_launch2()
    shared2 = {k: g[k] for k in ["wqm", "wom", "w0m", "w1m", "b0c", "b1c",
                                 "pmat"]}
    ltc = (np.arange(512)[:, None] < np.arange(512)[None, :]).astype(
        ml_dtypes.bfloat16).reshape(4, 128, 512)
    iota = np.tile(np.arange(64, dtype=np.float32), (128, 1))
    shared2.update(cktg=cktg, cvt=cvt, kis=kis, qi0=qi0, ltc=ltc, iota64=iota)
    in2 = [dict(shared2, x1=r1[c]["x1_out"], xh1=r1[c]["xh1_out"],
                cost=tabs[c][0], sint=tabs[c][1])
           for c in range(NCORE)]
    r2 = run_bass_kernel_spmd(nc2, in2, core_ids=list(range(NCORE))).results

    y = np.stack([r["y_out"] for r in r2])
    out = y.transpose(0, 3, 1, 2).reshape(1, NCORE * T, D)
    return np.ascontiguousarray(out.astype(np.float32))
